# version 02
# baseline (speedup 1.0000x reference)
import contextlib

import numpy as np
import ml_dtypes

import concourse.bass as bass
import concourse.mybir as mybir
from concourse.bass_utils import run_bass_kernel_spmd

F32 = mybir.dt.float32
BF16 = mybir.dt.bfloat16
U8 = mybir.dt.uint8
AF = mybir.ActivationFunctionType
ALU = mybir.AluOpType
AX = mybir.AxisListType
BF = ml_dtypes.bfloat16

CONSTS = (1e-5, -0.5, -1.0, 1.0, 2.0, 0.5, 3.0, 4.0)

NCORES = 8
T = 2048
TB = 8192
D = 1024
NH = 16
NE = 8
HID = 2816
DI = 2048
DS = 16

ENGINES = ("sync", "tensor", "vector", "scalar", "gpsimd")


class KB:

    def __init__(self):
        self.nc = bass.Bass("TRN2", target_bir_lowering=False, debug=False)
        self.stack = contextlib.ExitStack()
        self.plan = []
        self.semv = {}
        self._pend = {e: [] for e in ENGINES}

    def sem(self, name):
        h = self.stack.enter_context(self.nc.semaphore(name))
        self.semv[h] = 0
        return h

    def sbuf(self, name, shape, dt):
        return self.stack.enter_context(self.nc.sbuf_tensor(name, shape, dt))

    _PSN = [0]

    def psum(self, shape, dt=F32):
        self._PSN[0] += 1
        return self.stack.enter_context(
            self.nc.psum_tensor(f"ps{self._PSN[0]}", shape, dt))

    def op(self, eng, fn, waits=(), inc=None, n=1, selfwait=None):
        w = list(self._pend[eng]) + list(waits)
        self._pend[eng] = []
        if selfwait is None:
            selfwait = eng in ("vector", "scalar")
        if selfwait and inc is not None and self.semv.get(inc, 0) > 0:
            w.append((inc, self.semv[inc]))
        incs = []
        if inc is not None:
            self.semv[inc] += n
            incs = [(inc, n)]
        self.plan.append((eng, fn, w, incs))

    def wait(self, eng, sem, val=None):
        self._pend[eng].append((sem, self.semv[sem] if val is None else val))

    def val(self, s):
        return self.semv[s]

    def fence(self):
        snap = [(s, v) for s, v in self.semv.items() if v > 0]
        for e in ENGINES:
            self._pend[e] = self._pend[e] + list(snap)

    def dma(self, q, out, in_, sem, waits=(), **kw):
        self.op(q, lambda e, o=out, i=in_, kw=kw: e.dma_start(out=o, in_=i, **kw),
                waits, inc=sem, n=16, selfwait=False)

    def dmat(self, q, out, in_, sem, waits=()):
        self.op(q, lambda e, o=out, i=in_: e.dma_start(out=o, in_=i, transpose=True),
                waits, inc=sem, n=16, selfwait=False)

    def mm(self, out, lhsT, rhs, start, stop, waits=(), inc=None):
        self.op("tensor",
                lambda e, o=out, l=lhsT, r=rhs, st=start, sp=stop:
                e.matmul(o, l, r, start=st, stop=sp),
                waits, inc=inc)

    def trans(self, out, in_, ident, waits=(), inc=None):
        self.op("tensor",
                lambda e, o=out, i=in_, d=ident: e.transpose(o, i, d),
                waits, inc=inc)

    def act(self, out, in_, func, bias=0.0, scale=1.0, waits=(), inc=None,
            accum_out=None):
        self.op("scalar",
                lambda e, o=out, i=in_, f=func, b=bias, s=scale, a=accum_out:
                e.activation(o, i, f, bias=b, scale=s, accum_out=a),
                waits, inc=inc)

    def v(self, fname, *args, waits=(), inc=None):
        self.op("vector",
                lambda e, fn=fname, a=args: getattr(e, fn)(*a),
                waits, inc=inc)

    def stt(self, out, in0, scalar, in1, op0, op1, waits=(), inc=None):
        self.op("vector",
                lambda e, o=out, a=in0, s=scalar, b=in1, o0=op0, o1=op1:
                e.scalar_tensor_tensor(o, a, s, b, o0, o1),
                waits, inc=inc)

    def emit(self):
        plan = self.plan
        with self.nc.Block() as block:
            for engname in ENGINES:
                def body(eng, engname=engname):
                    for (en, fn, waits, incs) in plan:
                        if en != engname:
                            continue
                        for (s, vv) in waits:
                            eng.wait_ge(s, vv)
                        inst = fn(eng)
                        for (s, a) in incs:
                            assert inst is not None, "inc on op with no inst"
                            inst.then_inc(s, a)
                getattr(block, engname)(body)
        return self.nc


def lay(total, **sizes):
    off, out = 0, {}
    for k, n in sizes.items():
        out[k] = off
        off += n
    assert off <= total, f"layout overflow: {off} > {total}"
    return out



def emit_layernorm_tm(kb, s_tiles, sq_scr, cols, n_tiles, DW, sems,
                      gain_col=None, bias_col=None):
    sDVE, sACT = sems
    inv = 1.0 / DW
    kb.wait("scalar", sDVE)
    kb.wait("vector", sACT)
    if hasattr(kb, "kcon_ld"):
        kb.wait("scalar", kb.sLd_h, kb.kcon_ld)
    for i in range(n_tiles):
        s = s_tiles(i)
        kb.act(sq_scr, s, AF.Square, accum_out=cols[:, 1:2], inc=sACT)
        kb.v("reduce_sum", cols[:, 0:1], s, AX.X, inc=sDVE)
        kb.wait("vector", sACT)
        kb.v("tensor_scalar_mul", cols[:, 2:3], cols[:, 0:1], inv, inc=sDVE)
        kb.v("tensor_scalar_mul", cols[:, 3:4], cols[:, 1:2], inv, inc=sDVE)
        kb.v("tensor_mul", cols[:, 6:7], cols[:, 2:3], cols[:, 2:3], inc=sDVE)
        kb.v("tensor_sub", cols[:, 3:4], cols[:, 3:4], cols[:, 6:7], inc=sDVE)
        kb.wait("scalar", sDVE)
        kb.act(cols[:, 7:8], cols[:, 3:4], AF.Ln, bias=1e-5, inc=sACT)
        kb.act(cols[:, 4:5], cols[:, 7:8], AF.Exp, scale=-0.5, inc=sACT)
        kb.wait("vector", sACT)
        kb.v("tensor_mul", cols[:, 5:6], cols[:, 2:3], cols[:, 4:5], inc=sDVE)
        kb.v("tensor_scalar_mul", cols[:, 5:6], cols[:, 5:6], -1.0, inc=sDVE)
        kb.wait("scalar", sDVE)
        kb.act(s, s, AF.Identity, scale=cols[:, 4:5], bias=cols[:, 5:6], inc=sACT)
        if gain_col is not None:
            kb.act(s, s, AF.Identity, scale=gain_col, bias=bias_col, inc=sACT)
        kb.wait("vector", sACT)



FA_COLS = 27776
BA_COLS = 46080


def build_l1(moe=True):
    kb = KB()
    dram = lambda n, s, dt, k: kb.nc.dram_tensor(n, s, dt, kind=k).ap()

    xk = dram("xk", [TB, D], F32, "ExternalInput")
    wq = dram("wq", [D, D], F32, "ExternalInput")
    wk = dram("wk", [D, D], F32, "ExternalInput")
    wv = dram("wv", [D, D], F32, "ExternalInput")
    wp = dram("wp", [D, D], F32, "ExternalInput")
    bq_c = dram("bq_c", [128, 8], F32, "ExternalInput")
    idf_d = dram("idf", [128, 128], F32, "ExternalInput")
    ckon_d = dram("ckon", [128, 8], F32, "ExternalInput")
    bk_r = dram("bk_r", [1, D], F32, "ExternalInput")
    bv_r = dram("bv_r", [1, D], F32, "ExternalInput")
    bp_r = dram("bp_r", [1, D], F32, "ExternalInput")
    id128b = dram("id128b", [128, 128], BF16, "ExternalInput")
    if moe:
        gwT = dram("gwT", [D, NE], F32, "ExternalInput")
        eb_r = dram("eb_r", [1, NE], F32, "ExternalInput")
        w1h = dram("w1h", [NE, D, HID], F32, "ExternalInput")
        w2h = dram("w2h", [NE, HID, D], F32, "ExternalInput")
        w3h = dram("w3h", [NE, D, HID], F32, "ExternalInput")
        x1hb = dram("x1hb", [128, 8 * T], BF16, "Internal")
        gemh = dram("gemh", [NE, T], F32, "Internal")
    xout = dram("xout", [T, D], F32, "ExternalOutput")
    xout_bf = dram("xout_bf", [T, D], BF16, "ExternalOutput")
    xkbf = dram("xkbf", [TB, D], BF16, "Internal")

    sLd = kb.sem("sLd")
    kb.sLd_h = sLd
    sCast = kb.sem("sCast")
    sTd = kb.sem("sTd")
    sPE = kb.sem("sPE")
    sDVE = kb.sem("sDVE")
    sACT = kb.sem("sACT")
    sSt = kb.sem("sSt")
    sems = (sDVE, sACT)

    bq_cols = kb.sbuf("bq_cols", [128, 8], F32)
    ones1 = kb.sbuf("ones1", [1, 128], F32)
    idb = kb.sbuf("idb", [128, 128], BF16)
    idf = kb.sbuf("idf_sb", [128, 128], F32)
    cols = kb.sbuf("cols", [128, 8], F32)
    kvbd = kb.sbuf("kvbd", [128, 8 * 128], BF16)

    FA = kb.sbuf("FA", [128, FA_COLS], F32)
    BA = kb.sbuf("BA", [128, BA_COLS], BF16)
    fa = lambda o, n: FA[:, o:o + n]
    ba = lambda o, n: BA[:, o:o + n]

    LF = lay(FA_COLS, xr=16 * D, bbc=3 * D, brow=3 * D, sq=D,
             s1=512, s2=512, gb=T, gsm=128, x1cf=D)
    LF["gem"] = LF["brow"]

    pA = kb.psum([128, 512])
    pB = kb.psum([128, 512])
    pC = kb.psum([128, 1024])
    pKV = kb.psum([128, 512])
    pT = kb.psum([128, 128], BF16)
    pTF = kb.psum([128, 128], F32)
    pR = kb.psum([128, 512])

    kcon = kb.sbuf("kcon", [128, 8], F32)
    kb.dma("sync", kcon[:], ckon_d, sLd)
    for j, cv in enumerate(CONSTS):
        kb.nc.const_aps.register(F32, cv, kcon[:, j:j + 1])
    kb.dma("sync", bq_cols[:], bq_c, sLd)
    kb.dma("sync", FA[0:1, LF["brow"]:LF["brow"] + D], bk_r, sLd)
    kb.dma("sync", FA[0:1, LF["brow"] + D:LF["brow"] + 2 * D], bv_r, sLd)
    kb.dma("sync", FA[0:1, LF["brow"] + 2 * D:LF["brow"] + 3 * D], bp_r, sLd)
    kb.dma("sync", idb[:], id128b, sLd)
    kb.dma("sync", idf[:], idf_d, sLd)
    kb.kcon_ld = kb.val(sLd)
    kb.v("memset", ones1[:], 1.0, inc=sDVE)

    xk_f = xk.rearrange("(n p) d -> n p d", p=128)
    xkbf_f = xkbf.rearrange("(n p) d -> n p d", p=128)
    for ch in range(8):
        boff = (ch % 2) * TB
        w = [(sCast, 16 * (16 * (ch - 2) + 16))] if ch >= 2 else []
        for j in range(8):
            kb.dma("gpsimd", BA[:, boff + j * D:boff + (j + 1) * D],
                   xk_f[ch * 8 + j], sCast, waits=w if j == 0 else ())
        kb.wait("gpsimd", sCast)
        for j in range(8):
            kb.dma("gpsimd", xkbf_f[ch * 8 + j],
                   BA[:, boff + j * D:boff + (j + 1) * D], sCast)
        kb.wait("gpsimd", sCast)
    cast_xk = kb.val(sCast)

    kb.wait("tensor", sLd)
    kb.wait("tensor", sDVE)
    for j in range(6):
        kb.mm(pR[:], ones1[:, :],
              FA[0:1, LF["brow"] + j * 512:LF["brow"] + (j + 1) * 512],
              True, True, inc=sPE)
        kb.wait("vector", sPE)
        kb.v("tensor_copy", fa(LF["bbc"] + j * 512, 512), pR[:], inc=sDVE)
        kb.wait("tensor", sDVE)

    LB = lay(BA_COLS, xq=8 * T, wk=8 * D, wv=8 * D, k1=4 * D, vb=4 * D)
    kv_sb = fa(LF["s1"], 512)
    for wsrc, off in ((wk, LB["wk"]), (wv, LB["wv"])):
        for ci in range(8):
            kb.dma("gpsimd", ba(off + ci * D, D),
                   wsrc[ci * 128:(ci + 1) * 128, :], sCast)
    cast_w = kb.val(sCast)
    kb.wait("gpsimd", sCast)

    for q4 in range(4):
        for ci in range(8):
            w = []
            if q4 == 0 and ci == 0:
                w.append((sCast, cast_xk))
            if q4 > 0 and ci == 0:
                w.append((sPE, kb.val(sPE)))
            kb.dmat("scalar", ba(LB["xq"] + ci * T, T),
                    xkbf[q4 * T:(q4 + 1) * T, ci * 128:(ci + 1) * 128], sTd,
                    waits=w)
        kb.wait("scalar", sTd)
        kb.wait("tensor", sTd)
        if q4 == 0:
            kb.wait("tensor", sCast, cast_w)
        if q4 == 0:
            kb.v("memset", fa(LF["gb"], 512), 0.0, inc=sDVE)
        for g in range(16):
            for coh in range(2):
                for ci in range(8):
                    kb.mm(pA[:], ba(LB["xq"] + ci * T + g * 128, 128),
                          ba(LB["wk"] + ci * D + coh * 512, 512),
                          ci == 0, ci == 7, inc=sPE if ci == 7 else None)
                for ci in range(8):
                    kb.mm(pB[:], ba(LB["xq"] + ci * T + g * 128, 128),
                          ba(LB["wv"] + ci * D + coh * 512, 512),
                          ci == 0, ci == 7, inc=sPE if ci == 7 else None)
                kb.wait("vector", sPE)
                s1 = fa(LF["s1"], 512)
                s2 = fa(LF["s2"], 512)
                kb.stt(s1, pA[:], 1.0, fa(LF["bbc"] + coh * 512, 512),
                       ALU.mult, ALU.add, inc=sDVE)
                kb.stt(s2, pB[:], 1.0, fa(LF["bbc"] + D + coh * 512, 512),
                       ALU.mult, ALU.add, inc=sDVE)
                kb.wait("tensor", sDVE)
                kb.v("tensor_copy",
                     ba(LB["vb"] + (g % 4) * D + coh * 512, 512), s2, inc=sDVE)
                kb.wait("scalar", sDVE)
                kb.act(fa(LF["sq"], 512), s1, AF.Relu, inc=sACT)
                kb.wait("vector", sACT)
                kb.v("tensor_scalar_min", s1, s1, 0.0, inc=sDVE)
                kb.wait("scalar", sDVE)
                kb.act(s1, s1, AF.Exp, inc=sACT)
                kb.wait("vector", sACT)
                kb.v("tensor_add", s1, s1, fa(LF["sq"], 512), inc=sDVE)
                kb.v("tensor_copy",
                     ba(LB["k1"] + (g % 4) * D + coh * 512, 512), s1, inc=sDVE)
            if g % 4 == 3:
                kb.wait("tensor", sDVE)
                for hp in range(8):
                    for hh in range(2):
                        for t4 in range(4):
                            h = hp * 2 + hh
                            kb.mm(pKV[hh * 64:hh * 64 + 64, 0:64],
                                  ba(LB["k1"] + t4 * D + h * 64, 64),
                                  ba(LB["vb"] + t4 * D + h * 64, 64),
                                  t4 == 0, t4 == 3,
                                  inc=sPE if (t4 == 3 and hh == 1) else None)
                    kb.wait("vector", sPE)
                    kb.op("vector", lambda e, o=FA[:, LF["gb"] + hp * 64:LF["gb"] + hp * 64 + 64],
                          a=FA[:, LF["gb"] + hp * 64:LF["gb"] + hp * 64 + 64],
                          b=pKV[:, 0:64]: e.tensor_add(o, a, b),
                          waits=[(sDVE, kb.val(sDVE))], inc=sDVE)
                    kb.wait("tensor", sDVE)
    kv_done = kb.val(sPE)
    kb.wait("vector", sPE, kv_done)
    kb.v("memset", kvbd[:], 0.0, inc=sDVE)
    for h in range(NH):
        prow = (h % 2) * 64
        hp = h // 2
        dro = (h % 2) * 64
        kb.v("tensor_copy",
             kvbd[dro:dro + 64, hp * 128 + dro:hp * 128 + dro + 64],
             FA[prow:prow + 64, LF["gb"] + hp * 64:LF["gb"] + hp * 64 + 64],
             inc=sDVE)

    LQ = lay(BA_COLS, xq=8 * T, wq=8 * D, q1=8 * T)
    for ci in range(8):
        w = [(sPE, kv_done)] if ci == 0 else []
        kb.dma("gpsimd", ba(LQ["wq"] + ci * D, D),
               wq[ci * 128:(ci + 1) * 128, :], sCast, waits=w)
    kb.wait("gpsimd", sCast)
    for ci in range(8):
        w = [(sPE, kv_done)] if ci == 0 else []
        kb.dmat("scalar", ba(LQ["xq"] + ci * T, T),
                xkbf[0:T, ci * 128:(ci + 1) * 128], sTd, waits=w)
    kb.wait("scalar", sTd)
    kb.wait("tensor", sTd)
    kb.wait("tensor", sCast)
    for co in range(8):
        for tc in range(4):
            for ci in range(8):
                kb.mm(pA[:], ba(LQ["wq"] + ci * D + co * 128, 128),
                      ba(LQ["xq"] + ci * T + tc * 512, 512),
                      ci == 0, ci == 7, inc=sPE if ci == 7 else None)
            kb.wait("scalar", sPE)
            kb.wait("scalar", sDVE)
            kb.act(fa(LF["s1"], 512), pA[:], AF.Identity,
                   bias=bq_cols[:, co:co + 1], inc=sACT)
            kb.act(fa(LF["s2"], 512), pA[:], AF.Relu,
                   bias=bq_cols[:, co:co + 1], inc=sACT)
            kb.wait("vector", sACT)
            kb.v("tensor_scalar_min", fa(LF["s1"], 512), fa(LF["s1"], 512),
                 0.0, inc=sDVE)
            kb.wait("scalar", sDVE)
            kb.act(fa(LF["s1"], 512), fa(LF["s1"], 512), AF.Exp, inc=sACT)
            kb.wait("vector", sACT)
            kb.v("tensor_add", fa(LF["s1"], 512), fa(LF["s1"], 512),
                 fa(LF["s2"], 512), inc=sDVE)
            kb.v("tensor_copy", ba(LQ["q1"] + co * T + tc * 512, 512),
                 fa(LF["s1"], 512), inc=sDVE)
            kb.wait("tensor", sDVE)

    for ci in range(8):
        w = [(sPE, kb.val(sPE))] if ci == 0 else []
        kb.dma("gpsimd", ba(LQ["wq"] + ci * D, D),
               wp[ci * 128:(ci + 1) * 128, :], sCast, waits=w)
    kb.wait("gpsimd", sCast)
    kb.wait("sync", sLd)
    for i in range(16):
        kb.dma("sync", fa(LF["xr"] + i * D, D), xk[i * 128:(i + 1) * 128, :],
               sLd)
    kb.wait("sync", sLd)
    for hp in range(8):
        for tc in range(4):
            kb.mm(pA[:], kvbd[:, hp * 128:(hp + 1) * 128],
                  ba(LQ["q1"] + hp * T + tc * 512, 512), True, True, inc=sPE)
            kb.wait("vector", sPE)
            kb.v("tensor_copy", ba(LQ["q1"] + hp * T + tc * 512, 512), pA[:],
                 inc=sDVE)
            kb.wait("tensor", sDVE)
    kb.wait("tensor", sCast)
    kb.wait("vector", sLd)
    for tt in range(16):
        for coh in range(2):
            for ci in range(8):
                kb.mm(pC[:, coh * 512:(coh + 1) * 512],
                      ba(LQ["q1"] + ci * T + tt * 128, 128),
                      ba(LQ["wq"] + ci * D + coh * 512, 512),
                      ci == 0, ci == 7, inc=sPE if ci == 7 else None)
        kb.wait("vector", sPE)
        for coh in range(2):
            xo = LF["xr"] + tt * D + coh * 512
            kb.stt(fa(LF["s1"], 512), pC[:, coh * 512:(coh + 1) * 512], 1.0,
                   fa(LF["bbc"] + 2 * D + coh * 512, 512), ALU.mult, ALU.add,
                   inc=sDVE)
            kb.v("tensor_add", fa(xo, 512), fa(xo, 512), fa(LF["s1"], 512),
                 inc=sDVE)
        kb.wait("tensor", sDVE)
    emit_layernorm_tm(kb, lambda i: fa(LF["xr"] + i * D, D),
                      fa(LF["sq"], D), cols, 16, D, sems)

    LC = lay(BA_COLS, xq=8 * T, wq=8 * D, x1cb=8 * T, stage=D)
    assert LC["x1cb"] == LQ["q1"]
    for i in range(16):
        kb.wait("vector", sACT)
        kb.v("tensor_copy", ba(LC["stage"], D), fa(LF["xr"] + i * D, D),
             inc=sDVE)
        kb.wait("tensor", sDVE)
        for ci in range(8):
            kb.trans(pT[:], ba(LC["stage"] + ci * 128, 128), idb[:], inc=sPE)
            kb.wait("vector", sPE)
            kb.v("tensor_copy", ba(LC["x1cb"] + ci * T + i * 128, 128), pT[:],
                 inc=sDVE)
            kb.wait("tensor", sDVE)

    if moe:
        kb.wait("gpsimd", sDVE)
        kb.dma("gpsimd", x1hb, BA[:, LC["x1cb"]:LC["x1cb"] + 8 * T], sCast)
        kb.wait("gpsimd", sCast)
        build_moe_l1(kb, (gwT, eb_r, w1h, w2h, w3h, x1hb, gemh),
                     (FA, BA, LF, LC), (pA, pB, pC, pR, pTF),
                     (sLd, sCast, sTd, sPE, sDVE, sACT), (ones1, idf, cols))
        kb.fence()
        emit_layernorm_tm(kb, lambda i: fa(LF["xr"] + i * D, D),
                          fa(LF["sq"], D), cols, 16, D, sems)

    for i in range(16):
        kb.wait("vector", sACT)
        if i >= 1:
            kb.wait("vector", sSt, 32 * i)
        kb.v("tensor_copy", ba(LC["stage"], D), fa(LF["xr"] + i * D, D),
             inc=sDVE)
        kb.wait("gpsimd", sDVE)
        kb.dma("gpsimd", xout[i * 128:(i + 1) * 128, :],
               fa(LF["xr"] + i * D, D), sSt)
        kb.dma("gpsimd", xout_bf[i * 128:(i + 1) * 128, :],
               ba(LC["stage"], D), sSt)
        kb.wait("gpsimd", sSt)
    kb.op("gpsimd", lambda e: e.memset(cols[0:1, 0:1], 0.0),
          waits=[(sSt, kb.val(sSt))])

    kb.emit()
    return kb


def build_moe_l1(kb, drams, mem, psums, sems6, consts):
    gwT, eb_r, w1h, w2h, w3h, x1hb, gemh = drams
    FA, BA, LF, LC = mem
    pA, pB, pC, pR, pTF = psums
    sLd, sCast, sTd, sPE, sDVE, sACT = sems6
    ones1, idf, cols = consts
    fa = lambda o, n: FA[:, o:o + n]
    ba = lambda o, n: BA[:, o:o + n]
    THIRDS = ((0, 8), (8, 15), (15, 22))

    GS = LF["gsm"]
    gcol = FA[:, GS:GS + 8]
    key = FA[:, GS + 8:GS + 16]
    msk = FA[:, GS + 16:GS + 24]
    ebb = FA[:, GS + 24:GS + 32]
    neg = FA[:, GS + 32:GS + 40]
    gtm = FA[:, GS + 40:GS + 48]
    ebr = FA[0:1, GS + 48:GS + 56]

    gw_f = FA[:, LF["gsm"] + 64:LF["gsm"] + 64 + 64]
    kb.wait("sync", sLd)
    kb.dma("sync", ebr, eb_r, sLd)
    for ci in range(8):
        kb.dma("sync", FA[:, LF["gsm"] + 64 + ci * NE:LF["gsm"] + 64 + (ci + 1) * NE],
               gwT[ci * 128:(ci + 1) * 128, :], sLd)
    kb.wait("sync", sLd)
    kb.v("memset", neg[:], -1e30, inc=sDVE)
    kb.wait("tensor", sLd)
    kb.wait("tensor", sDVE)
    kb.mm(pR[:, 0:8], ones1[:, :], ebr, True, True, inc=sPE)
    kb.wait("vector", sPE)
    kb.v("tensor_copy", ebb[:], pR[:, 0:8], inc=sDVE)
    kb.wait("tensor", sDVE)
    for tt in range(16):
        for ci in range(8):
            kb.trans(pTF[:], FA[:, LF["xr"] + tt * D + ci * 128:LF["xr"] + tt * D + ci * 128 + 128],
                     idf[:], inc=sPE)
            kb.wait("vector", sPE)
            kb.op("vector", lambda e, o=FA[:, LF["x1cf"] + ci * 128:LF["x1cf"] + (ci + 1) * 128],
                  p=pTF[:]: e.tensor_copy(o, p), inc=sDVE)
            kb.wait("tensor", sDVE)
        for ci in range(8):
            kb.mm(pR[:, 0:8],
                  FA[:, LF["x1cf"] + ci * 128:LF["x1cf"] + (ci + 1) * 128],
                  gw_f[:, ci * NE:(ci + 1) * NE],
                  ci == 0, ci == 7, inc=sPE if ci == 7 else None)
        kb.wait("vector", sPE)
        kb.op("vector", lambda e, o=gcol[:, 0:1], p=pR[:, 0:8]:
              e.reduce_max(o, p, AX.X), inc=sDVE)
        kb.v("tensor_scalar_mul", gcol[:, 1:2], gcol[:, 0:1], -1.0, inc=sDVE)
        kb.wait("scalar", sDVE)
        kb.op("scalar", lambda e, o=gtm[:], p=pR[:, 0:8], b=gcol[:, 1:2]:
              e.activation(o, p, AF.Exp, bias=b), inc=sACT)
        kb.wait("vector", sACT)
        kb.wait("tensor", sACT)
        kb.op("vector", lambda e, o=gcol[:, 2:3], i=gtm[:]:
              e.reduce_sum(o, i, AX.X), inc=sDVE)
        kb.v("reciprocal", gcol[:, 3:4], gcol[:, 2:3], inc=sDVE)
        kb.op("vector", lambda e, o=gtm[:], i=gtm[:], m=gcol[:, 3:4]:
              e.tensor_scalar_mul(o, i, m), inc=sDVE)
        kb.v("tensor_add", key[:], gtm[:], ebb[:], inc=sDVE)
        kb.op("vector", lambda e, o=gcol[:, 4:5], i=key[:]:
              e.reduce_max(o, i, AX.X), inc=sDVE)
        kb.op("vector", lambda e, o=msk[:], i=key[:], m=gcol[:, 4:5]:
              e.tensor_single_scalar(o, i, m, ALU.is_equal), inc=sDVE)
        kb.v("tensor_scalar_mul", msk[:], msk[:], -1e30, inc=sDVE)
        kb.v("tensor_add", msk[:], msk[:], key[:], inc=sDVE)
        kb.op("vector", lambda e, o=gcol[:, 5:6], i=msk[:]:
              e.reduce_max(o, i, AX.X), inc=sDVE)
        kb.op("vector", lambda e, o=key[:], i=key[:], m=gcol[:, 5:6]:
              e.tensor_single_scalar(o, i, m, ALU.is_ge), inc=sDVE)
        kb.v("tensor_mul", gtm[:], gtm[:], key[:], inc=sDVE)
        kb.wait("tensor", sDVE)
        kb.trans(pTF[0:8, :], gtm[:], idf[:], inc=sPE)
        kb.wait("vector", sPE)
        kb.op("vector",
              lambda e, o=FA[0:8, LF["gem"] + tt * 128:LF["gem"] + (tt + 1) * 128],
              p=pTF[0:8, :]: e.tensor_copy(o, p), inc=sDVE)
        kb.wait("tensor", sDVE)
    kb.wait("gpsimd", sDVE)
    kb.dma("gpsimd", gemh, FA[0:8, LF["gem"]:LF["gem"] + T], sCast)
    kb.wait("gpsimd", sCast)

    LM = lay(BA_COLS, xe=8 * T, w1t=8 * 1024, w3t=8 * 1024, w2t=8 * 1024,
             hsb=8 * 512)
    gemv = kb.val(sCast)
    for e in range(NE):
        kb.wait("gpsimd", sPE)
        kb.dma("gpsimd", BA[:, LM["xe"]:LM["xe"] + 8 * T], x1hb, sCast)
        kb.wait("gpsimd", sCast)
        kb.wait("sync", sLd)
        kb.wait("sync", sPE)
        kb.dma("sync", FA[0:1, LF["gem"]:LF["gem"] + T], gemh[e:e + 1, :],
               sLd, waits=[(sCast, gemv)])
        kb.wait("tensor", sLd)
        for tc in range(4):
            kb.mm(pR[:], ones1[:, :],
                  FA[0:1, LF["gem"] + tc * 512:LF["gem"] + tc * 512 + 512],
                  True, True, inc=sPE)
            kb.wait("vector", sPE)
            kb.v("tensor_copy", fa(LF["gb"] + tc * 512, 512), pR[:], inc=sDVE)
            kb.wait("tensor", sDVE)
        kb.wait("vector", sCast)
        for ci in range(8):
            for tc in range(4):
                sl = LM["xe"] + ci * T + tc * 512
                kb.op("vector", lambda e_, o=ba(sl, 512), a=ba(sl, 512),
                      b=fa(LF["gb"] + tc * 512, 512):
                      e_.tensor_mul(o, a, b), inc=sDVE)
        kb.wait("tensor", sDVE)

        for (hc0, hc1) in THIRDS:
            nb = hc1 - hc0
            kb.wait("gpsimd", sPE)
            for ci in range(8):
                kb.dma("gpsimd", ba(LM["w1t"] + ci * 1024, nb * 128),
                       w1h[e, ci * 128:(ci + 1) * 128, hc0 * 128:hc1 * 128],
                       sCast)
                kb.dma("gpsimd", ba(LM["w3t"] + ci * 1024, nb * 128),
                       w3h[e, ci * 128:(ci + 1) * 128, hc0 * 128:hc1 * 128],
                       sCast)
            for hl in range(nb):
                kb.dma("gpsimd", ba(LM["w2t"] + hl * 1024, 1024),
                       w2h[e, (hc0 + hl) * 128:(hc0 + hl) * 128 + 128, :],
                       sCast)
            kb.wait("gpsimd", sCast)
            kb.wait("tensor", sCast)
            for tc in range(4):
                for hl in range(nb):
                    for pt, woff in ((pA, LM["w1t"]), (pB, LM["w3t"])):
                        for ci in range(8):
                            kb.mm(pt[:],
                                  ba(woff + ci * 1024 + hl * 128, 128),
                                  ba(LM["xe"] + ci * T + tc * 512, 512),
                                  ci == 0, ci == 7,
                                  inc=sPE if ci == 7 else None)
                    kb.wait("scalar", sPE)
                    kb.act(fa(LF["s1"], 512), pA[:], AF.Silu, inc=sACT)
                    kb.wait("vector", sACT)
                    kb.wait("vector", sPE)
                    kb.op("vector", lambda e_, o=ba(LM["hsb"] + hl * 512, 512),
                          a=fa(LF["s1"], 512), b=pB[:]:
                          e_.tensor_mul(o, a, b), inc=sDVE)
                    kb.wait("tensor", sDVE)
                for ttl in range(4):
                    tt = tc * 4 + ttl
                    for coh in range(2):
                        for hl in range(nb):
                            kb.mm(pC[:, coh * 512:(coh + 1) * 512],
                                  ba(LM["hsb"] + hl * 512 + ttl * 128, 128),
                                  ba(LM["w2t"] + hl * 1024 + coh * 512, 512),
                                  hl == 0, hl == nb - 1,
                                  inc=sPE if hl == nb - 1 else None)
                    kb.wait("vector", sPE)
                    for coh in range(2):
                        xo = LF["xr"] + tt * D + coh * 512
                        kb.op("vector", lambda e_, o=fa(xo, 512),
                              a=fa(xo, 512), b=pC[:, coh * 512:(coh + 1) * 512]:
                              e_.tensor_add(o, a, b), inc=sDVE)
                    kb.wait("tensor", sDVE)



def build_l2a():
    kb = KB()
    dram = lambda n, s, dt, k: kb.nc.dram_tensor(n, s, dt, kind=k).ap()
    x2c = dram("x2c", [D, TB], BF16, "ExternalInput")
    winx = dram("winx", [D, 512], F32, "ExternalInput")
    winz = dram("winz", [D, 512], F32, "ExternalInput")
    cw = dram("cw", [128, 16], F32, "ExternalInput")
    cb = dram("cb", [128, 4], F32, "ExternalInput")
    u_b = dram("u_b", [512, TB], BF16, "ExternalOutput")
    u_f = dram("u_f", [512, TB], F32, "ExternalOutput")
    z_f = dram("z_f", [512, TB], F32, "ExternalOutput")

    sLd = kb.sem("sLd"); sCast = kb.sem("sCast"); sPE = kb.sem("sPE")
    sDVE = kb.sem("sDVE"); sACT = kb.sem("sACT"); sSt = kb.sem("sSt")

    cw_sb = kb.sbuf("cw_sb", [128, 16], F32)
    cb_sb = kb.sbuf("cb_sb", [128, 4], F32)
    wx_sb = kb.sbuf("wx_sb", [128, 8 * 512], BF16)
    wz_sb = kb.sbuf("wz_sb", [128, 8 * 512], BF16)
    xc = kb.sbuf("xc", [128, 2 * 8 * T], BF16)
    xs = kb.sbuf("xs", [128, 4 * (T + 3)], F32)
    ua = kb.sbuf("ua", [128, T], F32)
    us = kb.sbuf("us", [128, T], F32)
    ub = kb.sbuf("ub", [128, T], BF16)
    zs = kb.sbuf("zs", [128, T], F32)
    pA = kb.psum([128, 512]); pB = kb.psum([128, 512])

    kb.dma("sync", cw_sb[:], cw, sLd)
    kb.dma("sync", cb_sb[:], cb, sLd)
    kb.wait("sync", sLd)
    for ci in range(8):
        kb.dma("gpsimd", wx_sb[:, ci * 512:(ci + 1) * 512],
               winx[ci * 128:(ci + 1) * 128, :], sCast)
        kb.dma("gpsimd", wz_sb[:, ci * 512:(ci + 1) * 512],
               winz[ci * 128:(ci + 1) * 128, :], sCast)
    kb.wait("gpsimd", sCast)
    for dt in range(4):
        kb.v("memset", xs[:, dt * (T + 3):dt * (T + 3) + 3], 0.0, inc=sDVE)

    for ch in range(4):
        par = ch % 2
        xcb = lambda ci: xc[:, par * 8 * T + ci * T:par * 8 * T + (ci + 1) * T]
        w = [(sPE, kb.val(sPE))] if ch >= 2 else []
        kb.wait("sync", sLd)
        for ci in range(8):
            kb.dma("sync", xcb(ci), x2c[ci * 128:(ci + 1) * 128,
                                        ch * T:(ch + 1) * T], sLd,
                   waits=w if ci == 0 else ())
        kb.wait("sync", sLd)
        kb.wait("tensor", sLd)
        if ch == 0:
            kb.wait("tensor", sCast, kb.val(sCast))
        for dt in range(4):
            xso = dt * (T + 3)
            for t4 in range(4):
                for ci in range(8):
                    kb.mm(pA[:], wx_sb[:, ci * 512 + dt * 128:ci * 512 + dt * 128 + 128],
                          xcb(ci)[:, t4 * 512:(t4 + 1) * 512],
                          ci == 0, ci == 7, inc=sPE if ci == 7 else None)
                for ci in range(8):
                    kb.mm(pB[:], wz_sb[:, ci * 512 + dt * 128:ci * 512 + dt * 128 + 128],
                          xcb(ci)[:, t4 * 512:(t4 + 1) * 512],
                          ci == 0, ci == 7, inc=sPE if ci == 7 else None)
                kb.wait("vector", sPE)
                kb.v("tensor_copy", xs[:, xso + 3 + t4 * 512:xso + 3 + (t4 + 1) * 512],
                     pA[:], inc=sDVE)
                kb.v("tensor_copy", zs[:, t4 * 512:(t4 + 1) * 512], pB[:],
                     inc=sDVE)
                kb.wait("tensor", sDVE)
            kb.wait("gpsimd", sDVE)
            kb.wait("gpsimd", sSt)
            kb.dma("gpsimd", z_f[dt * 128:(dt + 1) * 128, ch * T:(ch + 1) * T],
                   zs[:], sSt)
            kb.op("vector", lambda e, o=ua[:], i=xs[:, xso:xso + T],
                  s=cw_sb[:, dt * 4:dt * 4 + 1]:
                  e.tensor_scalar_mul(o, i, s), inc=sDVE)
            for j in range(1, 4):
                kb.stt(ua[:], xs[:, xso + j:xso + j + T],
                       cw_sb[:, dt * 4 + j:dt * 4 + j + 1], ua[:],
                       ALU.mult, ALU.add, inc=sDVE)
            kb.v("tensor_copy", xs[:, xso:xso + 3],
                 xs[:, xso + T:xso + T + 3], inc=sDVE)
            kb.wait("scalar", sDVE)
            kb.wait("scalar", sSt, kb.val(sSt))
            kb.act(us[:], ua[:], AF.Silu, bias=cb_sb[:, dt:dt + 1], inc=sACT)
            kb.wait("vector", sACT)
            kb.v("tensor_copy", ub[:], us[:], inc=sDVE)
            kb.wait("gpsimd", sDVE)
            kb.wait("gpsimd", sSt)
            kb.dma("gpsimd", u_f[dt * 128:(dt + 1) * 128, ch * T:(ch + 1) * T],
                   us[:], sSt)
            kb.dma("gpsimd", u_b[dt * 128:(dt + 1) * 128, ch * T:(ch + 1) * T],
                   ub[:], sSt)
            kb.wait("vector", sSt)
    kb.op("gpsimd", lambda e: e.memset(cb_sb[0:1, 0:1], 0.0),
          waits=[(sSt, kb.val(sSt))])
    kb.emit()
    return kb



def build_l2b():
    kb = KB()
    dram = lambda n, s, dt, k: kb.nc.dram_tensor(n, s, dt, kind=k).ap()
    uf = dram("uf", [DI, TB], BF16, "ExternalInput")
    wxT = dram("wxT", [DI, 544], F32, "ExternalInput")
    din = dram("din", [512, TB], BF16, "ExternalOutput")
    bc = dram("bc", [32, TB], F32, "ExternalOutput")

    sLd = kb.sem("sLd"); sCast = kb.sem("sCast"); sPE = kb.sem("sPE")
    sDVE = kb.sem("sDVE"); sACT = kb.sem("sACT"); sSt = kb.sem("sSt")

    wx_sb = kb.sbuf("wx_sb", [128, 16 * 544], BF16)
    TC2 = 1024
    uc = kb.sbuf("uc", [128, 2 * 16 * TC2], BF16)
    st1 = kb.sbuf("st1", [128, 512], BF16)
    st2 = kb.sbuf("st2", [32, 512], F32)
    pA = kb.psum([128, 512]); pB = kb.psum([32, 512])

    for ci in range(16):
        kb.dma("gpsimd", wx_sb[:, ci * 544:(ci + 1) * 544],
               wxT[ci * 128:(ci + 1) * 128, :], sCast)
    kb.wait("gpsimd", sCast)
    for ch in range(8):
        par = ch % 2
        ucb = lambda ci: uc[:, par * 16 * TC2 + ci * TC2:par * 16 * TC2 + (ci + 1) * TC2]
        w = [(sPE, kb.val(sPE))] if ch >= 2 else []
        kb.wait("sync", sLd)
        for ci in range(16):
            kb.dma("sync", ucb(ci), uf[ci * 128:(ci + 1) * 128,
                                       ch * TC2:(ch + 1) * TC2], sLd,
                   waits=w if ci == 0 else ())
        kb.wait("sync", sLd)
        kb.wait("tensor", sLd)
        if ch == 0:
            kb.wait("tensor", sCast, kb.val(sCast))
        for t2 in range(2):
            for co in range(4):
                for ci in range(16):
                    kb.mm(pA[:], wx_sb[:, ci * 544 + co * 128:ci * 544 + co * 128 + 128],
                          ucb(ci)[:, t2 * 512:(t2 + 1) * 512],
                          ci == 0, ci == 15, inc=sPE if ci == 15 else None)
                kb.wait("vector", sPE)
                kb.wait("vector", sSt, kb.val(sSt))
                kb.v("tensor_copy", st1[:], pA[:], inc=sDVE)
                kb.wait("tensor", sDVE)
                kb.wait("gpsimd", sDVE)
                kb.wait("gpsimd", sSt)
                kb.dma("gpsimd", din[co * 128:(co + 1) * 128,
                                     ch * TC2 + t2 * 512:ch * TC2 + (t2 + 1) * 512],
                       st1[:], sSt)
            for ci in range(16):
                kb.mm(pB[:], wx_sb[:, ci * 544 + 512:ci * 544 + 544],
                      ucb(ci)[:, t2 * 512:(t2 + 1) * 512],
                      ci == 0, ci == 15, inc=sPE if ci == 15 else None)
            kb.wait("vector", sPE)
            kb.wait("vector", sSt, kb.val(sSt))
            kb.v("tensor_copy", st2[:], pB[:], inc=sDVE)
            kb.wait("tensor", sDVE)
            kb.wait("gpsimd", sDVE)
            kb.wait("gpsimd", sSt)
            kb.dma("gpsimd", bc[:, ch * TC2 + t2 * 512:ch * TC2 + (t2 + 1) * 512],
                   st2[:], sSt)
    kb.op("gpsimd", lambda e: e.memset(st2[0:1, 0:1], 0.0),
          waits=[(sSt, kb.val(sSt))])
    kb.emit()
    return kb



def build_l2c():
    kb = KB()
    dram = lambda n, s, dt, k: kb.nc.dram_tensor(n, s, dt, kind=k).ap()
    dinf = dram("dinf", [DI, TB], BF16, "ExternalInput")
    bcf = dram("bcf", [32, TB], F32, "ExternalInput")
    wdtT = dram("wdtT", [DI, 512], F32, "ExternalInput")
    bdt_c = dram("bdt_c", [128, 4], F32, "ExternalInput")
    alog_c = dram("alog_c", [128, 64], F32, "ExternalInput")
    dcol = dram("dcol", [128, 4], F32, "ExternalInput")
    uf32 = dram("uf32", [512, TB], F32, "ExternalInput")
    zf32 = dram("zf32", [512, TB], F32, "ExternalInput")
    woT = dram("woT", [512, D], F32, "ExternalInput")
    ckon_d = dram("ckon", [128, 8], F32, "ExternalInput")
    ones_d = dram("ones_d", [1, 128], F32, "ExternalInput")
    po = dram("po", [D, TB], F32, "ExternalOutput")

    sLd = kb.sem("sLd"); sCast = kb.sem("sCast"); sPE = kb.sem("sPE")
    sDVE = kb.sem("sDVE"); sACT = kb.sem("sACT"); sSt = kb.sem("sSt")

    kcon = kb.sbuf("kcon", [128, 8], F32)
    kb.dma("sync", kcon[:], ckon_d, sLd)
    for j, cv in enumerate(CONSTS):
        kb.nc.const_aps.register(F32, cv, kcon[:, j:j + 1])
    ones1 = kb.sbuf("ones1", [1, 128], F32)
    kb.dma("sync", ones1[:], ones_d, sLd)
    bdt = kb.sbuf("bdt", [128, 4], F32)
    kb.dma("sync", bdt[:], bdt_c, sLd)
    alog = kb.sbuf("alog", [128, 64], F32)
    kb.dma("sync", alog[:], alog_c, sLd)
    dco = kb.sbuf("dco", [128, 4], F32)
    kb.dma("sync", dco[:], dcol, sLd)
    kb.wait("sync", sLd)
    con_ld = kb.val(sLd)

    wdt_sb = kb.sbuf("wdt_sb", [128, 16 * 512], BF16)
    wo_sb = kb.sbuf("wo_sb", [128, 4 * D], BF16)
    for ci in range(16):
        kb.dma("gpsimd", wdt_sb[:, ci * 512:(ci + 1) * 512],
               wdtT[ci * 128:(ci + 1) * 128, :], sCast)
    for ci in range(4):
        kb.dma("gpsimd", wo_sb[:, ci * D:(ci + 1) * D],
               woT[ci * 128:(ci + 1) * 128, :], sCast)
    kb.wait("gpsimd", sCast)
    wld = kb.val(sCast)

    ac = kb.sbuf("ac", [128, 64], F32)
    kb.wait("scalar", sLd, con_ld)
    kb.act(ac[:], alog[:], AF.Exp, inc=sACT)
    kb.wait("vector", sACT)
    kb.v("tensor_scalar_mul", ac[:], ac[:], -1.0, inc=sDVE)

    TC = 1024
    dinc = kb.sbuf("dinc", [128, 16 * TC], BF16)
    brows = kb.sbuf("brows", [1, 4 * TC], F32)
    uc = kb.sbuf("ucf", [128, 4 * TC], F32)
    zc = kb.sbuf("zcf", [128, 4 * TC], F32)
    delta = kb.sbuf("delta", [128, 4 * TC], F32)
    du = kb.sbuf("du", [128, 4 * TC], F32)
    yac = kb.sbuf("yac", [128, 4 * TC], F32)
    bb = kb.sbuf("bb", [128, TC], F32)
    cbr = kb.sbuf("cbr", [128, TC], F32)
    dA = kb.sbuf("dA", [128, TC], F32)
    dBu = kb.sbuf("dBu", [128, TC], F32)
    sseq = kb.sbuf("sseq", [128, TC], F32)
    s1 = kb.sbuf("s1c", [128, 512], F32)
    s2 = kb.sbuf("s2c", [128, 512], F32)
    y2b = kb.sbuf("y2b", [128, 4 * TC], BF16)
    stc = kb.sbuf("stc", [128, 64], F32)
    pst = kb.sbuf("pst", [128, 512], F32)
    pA = kb.psum([128, 512]); pB = kb.psum([128, 512])
    pC = kb.psum([128, 512]); pD = kb.psum([128, 512])

    kb.v("memset", stc[:], 0.0, inc=sDVE)

    for ch in range(8):
        par = ch % 2
        dcb = lambda ci: dinc[:, ci * TC:(ci + 1) * TC]
        ucb = lambda dt: uc[:, dt * TC:(dt + 1) * TC]
        w = [(sPE, kb.val(sPE)), (sDVE, kb.val(sDVE))] if ch >= 1 else []
        kb.wait("sync", sLd)
        for ci in range(16):
            kb.dma("sync", dcb(ci), dinf[ci * 128:(ci + 1) * 128,
                                         ch * TC:(ch + 1) * TC], sLd,
                   waits=w if ci == 0 else ())
        for dt in range(4):
            kb.dma("sync", ucb(dt), uf32[dt * 128:(dt + 1) * 128,
                                         ch * TC:(ch + 1) * TC], sLd)
        kb.wait("sync", sLd)
        ld_ch = kb.val(sLd)
        kb.wait("tensor", sLd, ld_ch)
        if ch == 0:
            kb.wait("tensor", sCast, wld)
        for dt in range(4):
            for t2 in range(2):
                pp = (pA, pB)[t2]
                for ci in range(16):
                    kb.mm(pp[:], wdt_sb[:, ci * 512 + dt * 128:ci * 512 + dt * 128 + 128],
                          dcb(ci)[:, t2 * 512:(t2 + 1) * 512],
                          ci == 0, ci == 15, inc=sPE if ci == 15 else None)
                kb.wait("scalar", sPE)
                kb.wait("scalar", sDVE)
                do = dt * TC + t2 * 512
                kb.act(s1[:], pp[:], AF.Abs, bias=bdt[:, dt:dt + 1], inc=sACT)
                kb.act(s1[:], s1[:], AF.Exp, scale=-1.0, inc=sACT)
                kb.act(s1[:], s1[:], AF.Ln, bias=1.0, inc=sACT)
                kb.act(s2[:], pp[:], AF.Relu, bias=bdt[:, dt:dt + 1], inc=sACT)
                kb.wait("vector", sACT)
                kb.v("tensor_add", delta[:, do:do + 512], s1[:], s2[:],
                     inc=sDVE)
                kb.wait("tensor", sDVE)
            kb.wait("vector", sLd, ld_ch)
            kb.v("tensor_mul", du[:, dt * TC:(dt + 1) * TC],
                 delta[:, dt * TC:(dt + 1) * TC], ucb(dt), inc=sDVE)
        kb.v("memset", yac[:], 0.0, inc=sDVE)
        for n in range(DS):
            np_ = n % 2
            bro = np_ * 2 * TC
            kb.wait("sync", sPE)
            kb.wait("sync", sLd)
            kb.dma("sync", brows[0:1, bro:bro + TC],
                   bcf[n:n + 1, ch * TC:(ch + 1) * TC], sLd)
            kb.dma("sync", brows[0:1, bro + TC:bro + 2 * TC],
                   bcf[16 + n:17 + n, ch * TC:(ch + 1) * TC], sLd)
            kb.wait("sync", sLd)
            kb.wait("tensor", sLd)
            for hseg in range(2):
                kb.wait("tensor", sDVE)
                kb.mm(pC[:], ones1[:, :],
                      brows[0:1, bro + hseg * 512:bro + hseg * 512 + 512],
                      True, True, inc=sPE)
                kb.wait("vector", sPE)
                kb.v("tensor_copy", bb[:, hseg * 512:hseg * 512 + 512], pC[:],
                     inc=sDVE)
                kb.wait("tensor", sDVE)
                kb.mm(pC[:], ones1[:, :],
                      brows[0:1, bro + TC + hseg * 512:bro + TC + hseg * 512 + 512],
                      True, True, inc=sPE)
                kb.wait("vector", sPE)
                kb.v("tensor_copy", cbr[:, hseg * 512:hseg * 512 + 512], pC[:],
                     inc=sDVE)
            for dt in range(4):
                sl = slice(dt * TC, (dt + 1) * TC)
                kb.wait("scalar", sDVE)
                kb.act(dA[:], delta[:, sl], AF.Exp,
                       scale=ac[:, dt * 16 + n:dt * 16 + n + 1], inc=sACT)
                kb.v("tensor_mul", dBu[:], du[:, sl], bb[:], inc=sDVE)
                kb.wait("vector", sACT)
                kb.op("vector", lambda e, o=sseq[:], a=dA[:], b=dBu[:],
                      i=stc[:, dt * 16 + n:dt * 16 + n + 1]:
                      e.tensor_tensor_scan(o, a, b, i, ALU.mult, ALU.add),
                      inc=sDVE)
                kb.v("tensor_copy", stc[:, dt * 16 + n:dt * 16 + n + 1],
                     sseq[:, TC - 1:TC], inc=sDVE)
                kb.v("tensor_mul", sseq[:], sseq[:], cbr[:], inc=sDVE)
                kb.v("tensor_add", yac[:, sl], yac[:, sl], sseq[:], inc=sDVE)
        for dt in range(4):
            kb.dma("sync", zc[:, dt * TC:(dt + 1) * TC],
                   zf32[dt * 128:(dt + 1) * 128, ch * TC:(ch + 1) * TC], sLd)
        kb.wait("sync", sLd)
        kb.wait("scalar", sLd)
        kb.wait("scalar", sDVE)
        for dt in range(4):
            sl = slice(dt * TC, (dt + 1) * TC)
            kb.stt(yac[:, sl], ucb(dt), dco[:, dt:dt + 1], yac[:, sl],
                   ALU.mult, ALU.add, inc=sDVE)
            kb.wait("scalar", sDVE)
            kb.act(zc[:, sl], zc[:, sl], AF.Silu, inc=sACT)
            kb.wait("vector", sACT)
            kb.v("tensor_mul", yac[:, sl], yac[:, sl], zc[:, sl], inc=sDVE)
            kb.v("tensor_copy", y2b[:, sl], yac[:, sl], inc=sDVE)
        kb.wait("tensor", sDVE)
        for co in range(8):
            for t2 in range(2):
                pp = (pC, pD)[t2]
                for ci in range(4):
                    kb.mm(pp[:], wo_sb[:, ci * D + co * 128:ci * D + co * 128 + 128],
                          y2b[:, ci * TC + t2 * 512:ci * TC + t2 * 512 + 512],
                          ci == 0, ci == 3, inc=sPE if ci == 3 else None)
                kb.wait("vector", sPE)
                kb.wait("vector", sSt, kb.val(sSt))
                kb.v("tensor_copy", pst[:], pp[:], inc=sDVE)
                kb.wait("tensor", sDVE)
                kb.wait("gpsimd", sDVE)
                kb.wait("gpsimd", sSt)
                kb.dma("gpsimd", po[co * 128:(co + 1) * 128,
                                    ch * TC + t2 * 512:ch * TC + t2 * 512 + 512],
                       pst[:], sSt)
    kb.op("gpsimd", lambda e: e.memset(s1[0:1, 0:1], 0.0),
          waits=[(sSt, kb.val(sSt))])
    kb.emit()
    return kb



L3_FA = 9280
L3_BA = 77824


def build_l3():
    kb = KB()
    dram = lambda n, s, dt, k: kb.nc.dram_tensor(n, s, dt, kind=k).ap()
    p0 = dram("p0", [T, D], F32, "ExternalInput")
    p1 = dram("p1", [T, D], F32, "ExternalInput")
    p2 = dram("p2", [T, D], F32, "ExternalInput")
    p3 = dram("p3", [T, D], F32, "ExternalInput")
    x2 = dram("x2", [T, D], F32, "ExternalInput")
    lng = dram("lng", [1, D], F32, "ExternalInput")
    lnb = dram("lnb", [1, D], F32, "ExternalInput")
    k0t = dram("k0t", [D, 4096], U8, "ExternalInput")
    k0b = dram("k0b", [D, 4096], U8, "ExternalInput")
    k1 = dram("k1", [4096, 4096], U8, "ExternalInput")
    k2 = dram("k2", [4096, D], U8, "ExternalInput")
    scol = dram("scol", [128, 3], F32, "ExternalInput")
    id128b = dram("id128b", [128, 128], BF16, "ExternalInput")
    ckon_d = dram("ckon", [128, 8], F32, "ExternalInput")
    ones_d = dram("ones_d", [1, 128], F32, "ExternalInput")
    xo = dram("xo", [T, D], F32, "ExternalOutput")
    w0dh = dram("w0dh", [D, 4096], BF16, "Internal")
    w1dh = dram("w1dh", [4096, 4096], BF16, "Internal")
    w2dh = dram("w2dh", [4096, D], BF16, "Internal")
    x3fh = dram("x3fh", [T, D], F32, "Internal")
    x3bh = dram("x3bh", [D, T], BF16, "Internal")

    sLd = kb.sem("sLd")
    kb.sLd_h = sLd
    sPE = kb.sem("sPE")
    sDVE = kb.sem("sDVE")
    sACT = kb.sem("sACT")
    sSt = kb.sem("sSt")
    sems = (sDVE, sACT)

    kcon = kb.sbuf("kcon", [128, 8], F32)
    kb.dma("sync", kcon[:], ckon_d, sLd)
    for j, cv in enumerate(CONSTS):
        kb.nc.const_aps.register(F32, cv, kcon[:, j:j + 1])
    ones1 = kb.sbuf("ones1", [1, 128], F32)
    kb.dma("sync", ones1[:], ones_d, sLd)
    idb = kb.sbuf("idb", [128, 128], BF16)
    kb.dma("sync", idb[:], id128b, sLd)
    sc_sb = kb.sbuf("sc_sb", [128, 4], F32)
    kb.dma("sync", sc_sb[:, 0:3], scol, sLd)
    cols = kb.sbuf("cols", [128, 8], F32)
    gbc = kb.sbuf("gbc", [128, 2 * D], F32)

    FA = kb.sbuf("FA", [128, L3_FA], F32)
    BA = kb.sbuf("BA", [128, L3_BA], BF16)
    fa = lambda o, n: FA[:, o:o + n]
    ba = lambda o, n: BA[:, o:o + n]
    LF = lay(L3_FA, pl=5 * D, sq=D, s1=512, s2=512, xt=D)
    lr = FA[0:1, LF["pl"]:LF["pl"] + 2 * D]
    kb.dma("sync", lr[:, 0:D], lng, sLd)
    kb.dma("sync", lr[:, D:2 * D], lnb, sLd)
    kb.wait("sync", sLd)
    kb.kcon_ld = kb.val(sLd)
    LB = lay(L3_BA, h1=32 * 1024, h2=32 * 1024, x3h=8 * 1024, wst=4096)
    WB = LB["x3h"]
    pF = kb.psum([128, 3584])
    pT = kb.psum([128, 128], BF16)

    kb.wait("vector", sLd, kb.kcon_ld)
    kb.v("tensor_scalar_mul", sc_sb[:, 0:2], sc_sb[:, 0:2], 2.0, inc=sDVE)
    kb.wait("tensor", sLd, kb.kcon_ld)
    kb.wait("tensor", sDVE)
    for j in range(4):
        kb.mm(pF[:, 0:512], ones1[:, :], lr[:, j * 512:(j + 1) * 512],
              True, True, inc=sPE)
        kb.wait("vector", sPE)
        kb.v("tensor_copy", gbc[:, j * 512:(j + 1) * 512], pF[:, 0:512],
             inc=sDVE)
        kb.wait("tensor", sDVE)

    def build_w(src_u8, dst, rows, ncols, second=None):
        for r in range(rows // 128):
            kb.wait("gpsimd", sSt)
            kb.wait("gpsimd", sDVE)
            kb.dma("gpsimd", ba(LB["wst"], ncols),
                   src_u8[r * 128:(r + 1) * 128, :], sSt)
            if second is not None:
                kb.dma("gpsimd", ba(LB["h1"], ncols),
                       second[r * 128:(r + 1) * 128, :], sSt)
            kb.wait("gpsimd", sSt)
            kb.wait("vector", sSt, kb.val(sSt))
            if second is None:
                kb.op("vector", lambda e, o=ba(LB["wst"], ncols),
                      i=ba(LB["wst"], ncols):
                      e.tensor_scalar(o, i, 2.0, -1.0, ALU.mult, ALU.add),
                      inc=sDVE)
            else:
                kb.v("tensor_sub", ba(LB["wst"], ncols), ba(LB["wst"], ncols),
                     ba(LB["h1"], ncols), inc=sDVE)
                kb.v("tensor_scalar_mul", ba(LB["wst"], ncols),
                     ba(LB["wst"], ncols), 2.0, inc=sDVE)
            kb.wait("gpsimd", sDVE)
            kb.dma("gpsimd", dst[r * 128:(r + 1) * 128, :],
                   ba(LB["wst"], ncols), sSt)
            kb.wait("vector", sSt)

    build_w(k0t, w0dh, D, 4096, second=k0b)
    build_w(k1, w1dh, 4096, 4096)
    build_w(k2, w2dh, 4096, D)

    for i in range(16):
        kb.wait("sync", sLd)
        kb.wait("sync", sDVE)
        kb.wait("sync", sSt, kb.val(sSt))
        kb.dma("sync", fa(LF["pl"], D), p0[i * 128:(i + 1) * 128, :], sLd)
        kb.dma("sync", fa(LF["pl"] + D, D), p1[i * 128:(i + 1) * 128, :], sLd)
        kb.dma("sync", fa(LF["pl"] + 2 * D, D), p2[i * 128:(i + 1) * 128, :], sLd)
        kb.dma("sync", fa(LF["pl"] + 3 * D, D), p3[i * 128:(i + 1) * 128, :], sLd)
        kb.dma("sync", fa(LF["pl"] + 4 * D, D), x2[i * 128:(i + 1) * 128, :], sLd)
        kb.wait("sync", sLd)
        kb.wait("vector", sLd)
        m = fa(LF["pl"], D)
        kb.v("tensor_add", m, m, fa(LF["pl"] + D, D), inc=sDVE)
        kb.v("tensor_add", m, m, fa(LF["pl"] + 2 * D, D), inc=sDVE)
        kb.v("tensor_add", m, m, fa(LF["pl"] + 3 * D, D), inc=sDVE)
        kb.v("tensor_add", m, m, fa(LF["pl"] + 4 * D, D), inc=sDVE)
        emit_layernorm_tm(kb, lambda _i: m, fa(LF["sq"], D), cols, 1, D, sems)
        kb.wait("vector", sACT)
        kb.v("tensor_mul", m, m, gbc[:, 0:D], inc=sDVE)
        kb.v("tensor_add", m, m, gbc[:, D:2 * D], inc=sDVE)
        kb.v("tensor_add", m, m, fa(LF["pl"] + 4 * D, D), inc=sDVE)
        emit_layernorm_tm(kb, lambda _i: m, fa(LF["sq"], D), cols, 1, D, sems)
        kb.wait("gpsimd", sDVE)
        kb.wait("gpsimd", sACT)
        kb.wait("gpsimd", sSt)
        kb.dma("gpsimd", x3fh[i * 128:(i + 1) * 128, :], m, sSt)
        kb.wait("vector", sACT)
        kb.v("tensor_copy", ba(LB["wst"], D), m, inc=sDVE)
        kb.wait("tensor", sDVE)
        for ci in range(8):
            kb.trans(pT[:], ba(LB["wst"] + ci * 128, 128), idb[:], inc=sPE)
            kb.wait("vector", sPE)
            kb.v("tensor_copy",
                 ba(LB["x3h"] + ci * 1024 + (i % 8) * 128, 128), pT[:],
                 inc=sDVE)
            kb.wait("tensor", sDVE)
        if i % 8 == 7:
            half = i // 8
            kb.wait("gpsimd", sDVE)
            kb.wait("gpsimd", sSt)
            for ci in range(8):
                kb.dma("gpsimd", x3bh[ci * 128:(ci + 1) * 128,
                                      half * 1024:(half + 1) * 1024],
                       ba(LB["x3h"] + ci * 1024, 1024), sSt)
            kb.wait("gpsimd", sSt)
            kb.wait("vector", sSt)

    HT = 1024

    for half in range(2):
        h0 = half * HT
        kb.wait("sync", sSt, kb.val(sSt))
        kb.wait("sync", sPE)
        for ci in range(8):
            kb.dma("sync", ba(LB["x3h"] + ci * HT, HT),
                   x3bh[ci * 128:(ci + 1) * 128, h0:h0 + HT], sLd)
        kb.wait("sync", sLd)
        kb.wait("tensor", sLd)

        co = 0
        while co < 32:
            ng = min(3, 32 - co)
            kb.wait("sync", sPE)
            kb.wait("sync", sLd)
            for ci in range(8):
                kb.dma("sync", ba(LB["wst"] + ci * ng * 128, ng * 128),
                       w0dh[ci * 128:(ci + 1) * 128, co * 128:(co + ng) * 128],
                       sLd)
            kb.wait("sync", sLd)
            kb.wait("tensor", sLd)
            for ci in range(8):
                for g in range(ng):
                    for t2 in range(2):
                        kb.mm(pF[:, (g * 2 + t2) * 512:(g * 2 + t2) * 512 + 512],
                              ba(LB["wst"] + ci * ng * 128 + g * 128, 128),
                              ba(LB["x3h"] + ci * HT + t2 * 512, 512),
                              ci == 0, ci == 7,
                              inc=sPE if (ci == 7 and g == ng - 1 and t2 == 1)
                              else None)
            kb.wait("scalar", sPE)
            kb.wait("scalar", sDVE)
            for g in range(ng):
                for t2 in range(2):
                    kb.wait("scalar", sDVE)
                    kb.act(fa(LF["s1"], 512),
                           pF[:, (g * 2 + t2) * 512:(g * 2 + t2) * 512 + 512],
                           AF.Relu, scale=sc_sb[:, 0:1], inc=sACT)
                    kb.wait("vector", sACT)
                    kb.v("tensor_copy",
                         ba(LB["h1"] + (co + g) * HT + t2 * 512, 512),
                         fa(LF["s1"], 512), inc=sDVE)
            kb.wait("tensor", sDVE)
            co += ng

        co = 0
        while co < 32:
            ng = min(3, 32 - co)
            kb.wait("sync", sPE)
            kb.wait("sync", sLd)
            for ci in range(32):
                kb.dma("sync", BA[:, WB + ci * ng * 128:WB + (ci + 1) * ng * 128],
                       w1dh[ci * 128:(ci + 1) * 128, co * 128:(co + ng) * 128],
                       sLd)
            kb.wait("sync", sLd)
            kb.wait("tensor", sLd)
            for ci in range(32):
                for g in range(ng):
                    for t2 in range(2):
                        kb.mm(pF[:, (g * 2 + t2) * 512:(g * 2 + t2) * 512 + 512],
                              BA[:, WB + ci * ng * 128 + g * 128:WB + ci * ng * 128 + (g + 1) * 128],
                              ba(LB["h1"] + ci * HT + t2 * 512, 512),
                              ci == 0, ci == 31,
                              inc=sPE if (ci == 31 and g == ng - 1 and t2 == 1)
                              else None)
            kb.wait("scalar", sPE)
            kb.wait("scalar", sDVE)
            for g in range(ng):
                for t2 in range(2):
                    kb.wait("scalar", sDVE)
                    kb.act(fa(LF["s1"], 512),
                           pF[:, (g * 2 + t2) * 512:(g * 2 + t2) * 512 + 512],
                           AF.Relu, scale=sc_sb[:, 1:2], inc=sACT)
                    kb.wait("vector", sACT)
                    kb.v("tensor_copy",
                         ba(LB["h2"] + (co + g) * HT + t2 * 512, 512),
                         fa(LF["s1"], 512), inc=sDVE)
            kb.wait("tensor", sDVE)
            co += ng

        for ttg in range(0, 8, 3):
            ntt = min(3, 8 - ttg)
            for cig in range(0, 32, 11):
                nci = min(11, 32 - cig)
                kb.wait("sync", sPE)
                kb.wait("sync", sLd)
                for cl in range(nci):
                    kb.dma("sync", BA[:, WB + cl * D:WB + (cl + 1) * D],
                           w2dh[(cig + cl) * 128:(cig + cl + 1) * 128, :], sLd)
                kb.wait("sync", sLd)
                kb.wait("tensor", sLd)
                for cl in range(nci):
                    ci = cig + cl
                    for tt in range(ntt):
                        for coh in range(2):
                            kb.mm(pF[:, (tt * 2 + coh) * 512:(tt * 2 + coh) * 512 + 512],
                                  ba(LB["h2"] + ci * HT + (ttg + tt) * 128, 128),
                                  BA[:, WB + cl * D + coh * 512:WB + cl * D + coh * 512 + 512],
                                  ci == 0, ci == 31,
                                  inc=sPE if (cl == nci - 1 and tt == ntt - 1
                                              and coh == 1) else None)
            kb.wait("scalar", sPE)
            for tt in range(ntt):
                gi = half * 8 + ttg + tt
                kb.wait("sync", sDVE)
                kb.wait("sync", sSt, kb.val(sSt))
                kb.dma("sync", fa(LF["xt"], D), x3fh[gi * 128:(gi + 1) * 128, :],
                       sLd)
                kb.wait("sync", sLd)
                kb.wait("scalar", sDVE)
                for coh in range(2):
                    kb.act(fa(LF["s1"] if coh == 0 else LF["s2"], 512),
                           pF[:, (tt * 2 + coh) * 512:(tt * 2 + coh) * 512 + 512],
                           AF.Identity, scale=sc_sb[:, 2:3], inc=sACT)
                kb.wait("vector", sACT)
                kb.wait("vector", sLd)
                kb.v("tensor_add", fa(LF["xt"], 512), fa(LF["xt"], 512),
                     fa(LF["s1"], 512), inc=sDVE)
                kb.op("vector", lambda e, o=FA[:, LF["xt"] + 512:LF["xt"] + 1024],
                      a=FA[:, LF["xt"] + 512:LF["xt"] + 1024], b=fa(LF["s2"], 512):
                      e.tensor_add(o, a, b), inc=sDVE)
                emit_layernorm_tm(kb, lambda _i: fa(LF["xt"], D),
                                  fa(LF["sq"], D), cols, 1, D, sems)
                kb.wait("gpsimd", sACT)
                kb.wait("gpsimd", sSt)
                kb.dma("gpsimd", xo[gi * 128:(gi + 1) * 128, :], fa(LF["xt"], D),
                       sSt)
            kb.wait("tensor", sDVE)
            kb.wait("tensor", sACT)
    kb.op("gpsimd", lambda e: e.memset(cols[0:1, 0:1], 0.0),
          waits=[(sSt, kb.val(sSt))])
    kb.emit()
    return kb



_CACHE = {}


def _get(name, builder):
    if name not in _CACHE:
        _CACHE[name] = builder().nc
    return _CACHE[name]


def _launch(name, builder, maps):
    nc = _get(name, builder)
    return run_bass_kernel_spmd(nc, maps, core_ids=list(range(NCORES))).results


def _l1_in_maps(inputs, moe=True):
    x = np.asarray(inputs["L0_attn_x"])
    base = {
        "wq": np.ascontiguousarray(inputs["L0_attn_Wq"], np.float32),
        "wk": np.ascontiguousarray(inputs["L0_attn_Wk"], np.float32),
        "wv": np.ascontiguousarray(inputs["L0_attn_Wv"], np.float32),
        "wp": np.ascontiguousarray(inputs["L0_attn_Wp"], np.float32),
        "bq_c": np.ascontiguousarray(
            np.asarray(inputs["L0_attn_bq"], np.float32).reshape(8, 128).T),
        "bk_r": np.asarray(inputs["L0_attn_bk"], np.float32).reshape(1, D),
        "bv_r": np.asarray(inputs["L0_attn_bv"], np.float32).reshape(1, D),
        "bp_r": np.asarray(inputs["L0_attn_bp"], np.float32).reshape(1, D),
        "id128b": np.eye(128, dtype=np.float32).astype(BF),
        "idf": np.eye(128, dtype=np.float32),
        "ckon": np.tile(np.array([list(CONSTS)], np.float32), (128, 1)),
    }
    if moe:
        base.update({
            "gwT": np.ascontiguousarray(
                np.asarray(inputs["L0_ffn_gate_w"], np.float32).T),
            "eb_r": np.asarray(
                inputs["L0_ffn_expert_bias"], np.float32).reshape(1, NE),
            "w1h": np.ascontiguousarray(inputs["L0_ffn_w1"], np.float32),
            "w2h": np.ascontiguousarray(inputs["L0_ffn_w2"], np.float32),
            "w3h": np.ascontiguousarray(inputs["L0_ffn_w3"], np.float32),
        })
    maps = []
    for c in range(NCORES):
        b, off = c // 4, (c % 4) * T
        xb = np.asarray(x[b], dtype=np.float32)
        m = dict(base)
        m["xk"] = np.ascontiguousarray(
            np.concatenate([xb[off:], xb[:off]], axis=0))
        maps.append(m)
    return maps


def run_l1(inputs, moe=True):
    return _launch(("l1", moe), lambda m=moe: build_l1(moe=m),
                   _l1_in_maps(inputs, moe=moe))


def _col4(v):
    return np.ascontiguousarray(np.asarray(v, np.float32).reshape(4, 128).T)


def _l2a_in_maps(inputs, x2b_full):
    w_in = np.asarray(inputs["L1_attn_W_in"], np.float32)
    cw = np.asarray(inputs["L1_attn_conv_w"], np.float32)
    cb = np.asarray(inputs["L1_attn_conv_b"], np.float32)
    maps = []
    for c in range(NCORES):
        b, s = c // 4, c % 4
        sl = slice(s * 512, (s + 1) * 512)
        maps.append({
            "x2c": np.ascontiguousarray(x2b_full[b].T),
            "winx": np.ascontiguousarray(w_in[sl, :].T),
            "winz": np.ascontiguousarray(w_in[2048 + s * 512:2048 + (s + 1) * 512, :].T),
            "cw": np.ascontiguousarray(
                cw[sl, 0, :].reshape(4, 128, 4).transpose(1, 0, 2).reshape(128, 16)),
            "cb": _col4(cb[sl]),
        })
    return maps


def _l2b_in_maps(inputs, ub_full):
    w_x = np.asarray(inputs["L1_attn_W_x"], np.float32)
    maps = []
    for c in range(NCORES):
        b, s = c // 4, c % 4
        wsl = np.concatenate([w_x[s * 512:(s + 1) * 512], w_x[2048:2080]], axis=0)
        maps.append({
            "uf": np.ascontiguousarray(ub_full[b]),
            "wxT": np.ascontiguousarray(wsl.T),
        })
    return maps


def _l2c_in_maps(inputs, din_full, bc_b, r2a):
    w_dt = np.asarray(inputs["L1_attn_W_dt"], np.float32)
    b_dt = np.asarray(inputs["L1_attn_b_dt"], np.float32)
    alog = np.asarray(inputs["L1_attn_A_log"], np.float32)
    dd = np.asarray(inputs["L1_attn_D"], np.float32)
    w_out = np.asarray(inputs["L1_attn_W_out"], np.float32)
    ones_d = np.ones((1, 128), np.float32)
    ck = np.tile(np.array([list(CONSTS)], np.float32), (128, 1))
    maps = []
    for c in range(NCORES):
        b, s = c // 4, c % 4
        sl = slice(s * 512, (s + 1) * 512)
        maps.append({
            "dinf": np.ascontiguousarray(din_full[b]),
            "bcf": np.ascontiguousarray(bc_b[b]),
            "wdtT": np.ascontiguousarray(w_dt[sl, :].T),
            "bdt_c": _col4(b_dt[sl]),
            "alog_c": np.ascontiguousarray(
                alog[sl, :].reshape(4, 128, 16).transpose(1, 0, 2).reshape(128, 64)),
            "dcol": _col4(dd[sl]),
            "uf32": r2a[c]["u_f"],
            "zf32": r2a[c]["z_f"],
            "woT": np.ascontiguousarray(w_out[:, sl].T),
            "ckon": ck,
            "ones_d": ones_d,
        })
    return maps


def _l3_in_maps(inputs, po, r1):
    k0 = np.asarray(inputs["L1_ffn_k0"]).view(np.uint8)
    k1 = np.ascontiguousarray(np.asarray(inputs["L1_ffn_k1"]).view(np.uint8))
    k2 = np.ascontiguousarray(np.asarray(inputs["L1_ffn_k2"]).view(np.uint8))
    s0 = np.asarray(inputs["L1_ffn_s0"], np.float32).reshape(())
    s1 = np.asarray(inputs["L1_ffn_s1"], np.float32).reshape(())
    s2 = np.asarray(inputs["L1_ffn_s2"], np.float32).reshape(())
    scol = np.tile(np.stack([s0, s1, s2]).reshape(1, 3), (128, 1))
    lng = np.asarray(inputs["L1_attn_ln_g"], np.float32).reshape(1, D)
    lnb = np.asarray(inputs["L1_attn_ln_b"], np.float32).reshape(1, D)
    base = {
        "lng": lng, "lnb": lnb,
        "k0t": np.ascontiguousarray(k0[:D]),
        "k0b": np.ascontiguousarray(k0[D:]),
        "k1": k1, "k2": k2,
        "scol": np.ascontiguousarray(scol),
        "id128b": np.eye(128, dtype=np.float32).astype(BF),
        "ckon": np.tile(np.array([list(CONSTS)], np.float32), (128, 1)),
        "ones_d": np.ones((1, 128), np.float32),
    }
    maps = []
    for c in range(NCORES):
        b, off = c // 4, (c % 4) * T
        m = dict(base)
        for j in range(4):
            m[f"p{j}"] = np.ascontiguousarray(
                po[b * 4 + j][:, off:off + T].T)
        m["x2"] = r1[c]["xout"]
        maps.append(m)
    return maps


def kernel(**inputs):
    r1 = run_l1(inputs, moe=True)
    x2b_full = [np.concatenate([r1[b * 4 + j]["xout_bf"] for j in range(4)],
                               axis=0) for b in range(2)]
    r2a = _launch("l2a", build_l2a, _l2a_in_maps(inputs, x2b_full))
    ub_full = [np.concatenate([r2a[b * 4 + j]["u_b"] for j in range(4)], axis=0)
               for b in range(2)]
    r2b = _launch("l2b", build_l2b, _l2b_in_maps(inputs, ub_full))
    din_full = [np.concatenate([r2b[b * 4 + j]["din"] for j in range(4)], axis=0)
                for b in range(2)]
    bc_b = [r2b[b * 4]["bc"] for b in range(2)]
    r2c = _launch("l2c", build_l2c, _l2c_in_maps(inputs, din_full, bc_b, r2a))
    po = [r2c[c]["po"] for c in range(NCORES)]
    r3 = _launch("l3", build_l3, _l3_in_maps(inputs, po, r1))
    out = np.empty((2, TB, D), np.float32)
    for c in range(NCORES):
        b, off = c // 4, (c % 4) * T
        out[b, off:off + T] = r3[c]["xo"]
    return out


# version 03
# speedup vs baseline: 1.1522x; 1.1522x over previous
import contextlib

import numpy as np
import ml_dtypes

import concourse.bass as bass
import concourse.mybir as mybir
from concourse.bass_utils import run_bass_kernel_spmd

F32 = mybir.dt.float32
BF16 = mybir.dt.bfloat16
U8 = mybir.dt.uint8
AF = mybir.ActivationFunctionType
ALU = mybir.AluOpType
AX = mybir.AxisListType
BF = ml_dtypes.bfloat16

CONSTS = (1e-5, -0.5, -1.0, 1.0, 2.0, 0.5, 3.0, 4.0)

NCORES = 8
T = 2048
TB = 8192
D = 1024
NH = 16
NE = 8
HID = 2816
DI = 2048
DS = 16

ENGINES = ("sync", "tensor", "vector", "scalar", "gpsimd")


class KB:

    def __init__(self):
        self.nc = bass.Bass("TRN2", target_bir_lowering=False, debug=False)
        self.stack = contextlib.ExitStack()
        self.plan = []
        self.semv = {}
        self._pend = {e: [] for e in ENGINES}

    def sem(self, name):
        h = self.stack.enter_context(self.nc.semaphore(name))
        self.semv[h] = 0
        return h

    def sbuf(self, name, shape, dt):
        return self.stack.enter_context(self.nc.sbuf_tensor(name, shape, dt))

    _PSN = [0]

    def psum(self, shape, dt=F32):
        self._PSN[0] += 1
        return self.stack.enter_context(
            self.nc.psum_tensor(f"ps{self._PSN[0]}", shape, dt))

    def op(self, eng, fn, waits=(), inc=None, n=1, selfwait=None):
        w = list(self._pend[eng]) + list(waits)
        self._pend[eng] = []
        if selfwait is None:
            selfwait = eng in ("vector", "scalar")
        if selfwait and inc is not None and self.semv.get(inc, 0) > 0:
            w.append((inc, self.semv[inc]))
        incs = []
        if inc is not None:
            self.semv[inc] += n
            incs = [(inc, n)]
        self.plan.append((eng, fn, w, incs))

    def wait(self, eng, sem, val=None):
        self._pend[eng].append((sem, self.semv[sem] if val is None else val))

    def val(self, s):
        return self.semv[s]

    def fence(self):
        snap = [(s, v) for s, v in self.semv.items() if v > 0]
        for e in ENGINES:
            self._pend[e] = self._pend[e] + list(snap)

    def dma(self, q, out, in_, sem, waits=(), **kw):
        self.op(q, lambda e, o=out, i=in_, kw=kw: e.dma_start(out=o, in_=i, **kw),
                waits, inc=sem, n=16, selfwait=False)

    def dmat(self, q, out, in_, sem, waits=()):
        self.op(q, lambda e, o=out, i=in_: e.dma_start(out=o, in_=i, transpose=True),
                waits, inc=sem, n=16, selfwait=False)

    def mm(self, out, lhsT, rhs, start, stop, waits=(), inc=None):
        self.op("tensor",
                lambda e, o=out, l=lhsT, r=rhs, st=start, sp=stop:
                e.matmul(o, l, r, start=st, stop=sp),
                waits, inc=inc)

    def trans(self, out, in_, ident, waits=(), inc=None):
        self.op("tensor",
                lambda e, o=out, i=in_, d=ident: e.transpose(o, i, d),
                waits, inc=inc)

    def act(self, out, in_, func, bias=0.0, scale=1.0, waits=(), inc=None,
            accum_out=None):
        self.op("scalar",
                lambda e, o=out, i=in_, f=func, b=bias, s=scale, a=accum_out:
                e.activation(o, i, f, bias=b, scale=s, accum_out=a),
                waits, inc=inc)

    def v(self, fname, *args, waits=(), inc=None):
        self.op("vector",
                lambda e, fn=fname, a=args: getattr(e, fn)(*a),
                waits, inc=inc)

    def stt(self, out, in0, scalar, in1, op0, op1, waits=(), inc=None):
        self.op("vector",
                lambda e, o=out, a=in0, s=scalar, b=in1, o0=op0, o1=op1:
                e.scalar_tensor_tensor(o, a, s, b, o0, o1),
                waits, inc=inc)

    def emit(self):
        plan = self.plan
        with self.nc.Block() as block:
            for engname in ENGINES:
                def body(eng, engname=engname):
                    for (en, fn, waits, incs) in plan:
                        if en != engname:
                            continue
                        for (s, vv) in waits:
                            eng.wait_ge(s, vv)
                        inst = fn(eng)
                        for (s, a) in incs:
                            assert inst is not None, "inc on op with no inst"
                            inst.then_inc(s, a)
                getattr(block, engname)(body)
        return self.nc


def lay(total, **sizes):
    off, out = 0, {}
    for k, n in sizes.items():
        out[k] = off
        off += n
    assert off <= total, f"layout overflow: {off} > {total}"
    return out



def emit_layernorm_tm(kb, s_tiles, sq_scr, cols, n_tiles, DW, sems,
                      gain_col=None, bias_col=None):
    sDVE, sACT = sems
    inv = 1.0 / DW
    kb.wait("scalar", sDVE)
    kb.wait("vector", sACT)
    if hasattr(kb, "kcon_ld"):
        kb.wait("scalar", kb.sLd_h, kb.kcon_ld)
    for i in range(n_tiles):
        s = s_tiles(i)
        kb.act(sq_scr, s, AF.Square, accum_out=cols[:, 1:2], inc=sACT)
        kb.v("reduce_sum", cols[:, 0:1], s, AX.X, inc=sDVE)
        kb.wait("vector", sACT)
        kb.v("tensor_scalar_mul", cols[:, 2:3], cols[:, 0:1], inv, inc=sDVE)
        kb.v("tensor_scalar_mul", cols[:, 3:4], cols[:, 1:2], inv, inc=sDVE)
        kb.v("tensor_mul", cols[:, 6:7], cols[:, 2:3], cols[:, 2:3], inc=sDVE)
        kb.v("tensor_sub", cols[:, 3:4], cols[:, 3:4], cols[:, 6:7], inc=sDVE)
        kb.wait("scalar", sDVE)
        kb.act(cols[:, 7:8], cols[:, 3:4], AF.Ln, bias=1e-5, inc=sACT)
        kb.act(cols[:, 4:5], cols[:, 7:8], AF.Exp, scale=-0.5, inc=sACT)
        kb.wait("vector", sACT)
        kb.v("tensor_mul", cols[:, 5:6], cols[:, 2:3], cols[:, 4:5], inc=sDVE)
        kb.v("tensor_scalar_mul", cols[:, 5:6], cols[:, 5:6], -1.0, inc=sDVE)
        kb.wait("scalar", sDVE)
        kb.act(s, s, AF.Identity, scale=cols[:, 4:5], bias=cols[:, 5:6], inc=sACT)
        if gain_col is not None:
            kb.act(s, s, AF.Identity, scale=gain_col, bias=bias_col, inc=sACT)
        kb.wait("vector", sACT)



FA_COLS = 27776
BA_COLS = 46080


def build_l1(moe=True):
    kb = KB()
    dram = lambda n, s, dt, k: kb.nc.dram_tensor(n, s, dt, kind=k).ap()

    xk = dram("xk", [TB, D], F32, "ExternalInput")
    wq = dram("wq", [D, D], F32, "ExternalInput")
    wk = dram("wk", [D, D], F32, "ExternalInput")
    wv = dram("wv", [D, D], F32, "ExternalInput")
    wp = dram("wp", [D, D], F32, "ExternalInput")
    bq_c = dram("bq_c", [128, 8], F32, "ExternalInput")
    idf_d = dram("idf", [128, 128], F32, "ExternalInput")
    ckon_d = dram("ckon", [128, 8], F32, "ExternalInput")
    bk_r = dram("bk_r", [1, D], F32, "ExternalInput")
    bv_r = dram("bv_r", [1, D], F32, "ExternalInput")
    bp_r = dram("bp_r", [1, D], F32, "ExternalInput")
    id128b = dram("id128b", [128, 128], BF16, "ExternalInput")
    if moe:
        gwT = dram("gwT", [D, NE], F32, "ExternalInput")
        eb_r = dram("eb_r", [1, NE], F32, "ExternalInput")
        w1h = dram("w1h", [NE, D, HID], F32, "ExternalInput")
        w2h = dram("w2h", [NE, HID, D], F32, "ExternalInput")
        w3h = dram("w3h", [NE, D, HID], F32, "ExternalInput")
        x1hb = dram("x1hb", [128, 8 * T], BF16, "Internal")
        gemh = dram("gemh", [NE, T], F32, "Internal")
    xout = dram("xout", [T, D], F32, "ExternalOutput")
    xout_bf = dram("xout_bf", [T, D], BF16, "ExternalOutput")
    xkbf = dram("xkbf", [TB, D], BF16, "Internal")

    sLd = kb.sem("sLd")
    kb.sLd_h = sLd
    sCast = kb.sem("sCast")
    sTd = kb.sem("sTd")
    sPE = kb.sem("sPE")
    sDVE = kb.sem("sDVE")
    sACT = kb.sem("sACT")
    sSt = kb.sem("sSt")
    sems = (sDVE, sACT)

    bq_cols = kb.sbuf("bq_cols", [128, 8], F32)
    ones1 = kb.sbuf("ones1", [1, 128], F32)
    idb = kb.sbuf("idb", [128, 128], BF16)
    idf = kb.sbuf("idf_sb", [128, 128], F32)
    cols = kb.sbuf("cols", [128, 8], F32)
    kvbd = kb.sbuf("kvbd", [128, 8 * 128], BF16)

    FA = kb.sbuf("FA", [128, FA_COLS], F32)
    BA = kb.sbuf("BA", [128, BA_COLS], BF16)
    fa = lambda o, n: FA[:, o:o + n]
    ba = lambda o, n: BA[:, o:o + n]

    LF = lay(FA_COLS, xr=16 * D, bbc=3 * D, brow=3 * D, sq=D,
             s1=512, s2=512, gb=T, gsm=128, x1cf=D)
    LF["gem"] = LF["brow"]

    pA = kb.psum([128, 512])
    pB = kb.psum([128, 512])
    pC = kb.psum([128, 1024])
    pKV = kb.psum([128, 512])
    pT = kb.psum([128, 128], BF16)
    pTF = kb.psum([128, 128], F32)
    pR = kb.psum([128, 512])

    kcon = kb.sbuf("kcon", [128, 8], F32)
    kb.dma("sync", kcon[:], ckon_d, sLd)
    for j, cv in enumerate(CONSTS):
        kb.nc.const_aps.register(F32, cv, kcon[:, j:j + 1])
    kb.dma("sync", bq_cols[:], bq_c, sLd)
    kb.dma("sync", FA[0:1, LF["brow"]:LF["brow"] + D], bk_r, sLd)
    kb.dma("sync", FA[0:1, LF["brow"] + D:LF["brow"] + 2 * D], bv_r, sLd)
    kb.dma("sync", FA[0:1, LF["brow"] + 2 * D:LF["brow"] + 3 * D], bp_r, sLd)
    kb.dma("sync", idb[:], id128b, sLd)
    kb.dma("sync", idf[:], idf_d, sLd)
    kb.kcon_ld = kb.val(sLd)
    kb.v("memset", ones1[:], 1.0, inc=sDVE)

    xk_f = xk.rearrange("(n p) d -> n p d", p=128)
    xkbf_f = xkbf.rearrange("(n p) d -> n p d", p=128)
    for ch in range(8):
        boff = (ch % 2) * TB
        w = [(sCast, 16 * (16 * (ch - 2) + 16))] if ch >= 2 else []
        for j in range(8):
            kb.dma("gpsimd", BA[:, boff + j * D:boff + (j + 1) * D],
                   xk_f[ch * 8 + j], sCast, waits=w if j == 0 else ())
        kb.wait("gpsimd", sCast)
        for j in range(8):
            kb.dma("gpsimd", xkbf_f[ch * 8 + j],
                   BA[:, boff + j * D:boff + (j + 1) * D], sCast)
        kb.wait("gpsimd", sCast)
    cast_xk = kb.val(sCast)

    kb.wait("tensor", sLd)
    kb.wait("tensor", sDVE)
    for j in range(6):
        kb.mm(pR[:], ones1[:, :],
              FA[0:1, LF["brow"] + j * 512:LF["brow"] + (j + 1) * 512],
              True, True, inc=sPE)
        kb.wait("vector", sPE)
        kb.v("tensor_copy", fa(LF["bbc"] + j * 512, 512), pR[:], inc=sDVE)
        kb.wait("tensor", sDVE)

    LB = lay(BA_COLS, xq=8 * T, wk=8 * D, wv=8 * D, k1=4 * D, vb=4 * D)
    kv_sb = fa(LF["s1"], 512)
    for wsrc, off in ((wk, LB["wk"]), (wv, LB["wv"])):
        for ci in range(8):
            kb.dma("gpsimd", ba(off + ci * D, D),
                   wsrc[ci * 128:(ci + 1) * 128, :], sCast)
    cast_w = kb.val(sCast)
    kb.wait("gpsimd", sCast)

    for q4 in range(4):
        for ci in range(8):
            w = []
            if q4 == 0 and ci == 0:
                w.append((sCast, cast_xk))
            if q4 > 0 and ci == 0:
                w.append((sPE, kb.val(sPE)))
            kb.dmat("scalar", ba(LB["xq"] + ci * T, T),
                    xkbf[q4 * T:(q4 + 1) * T, ci * 128:(ci + 1) * 128], sTd,
                    waits=w)
        kb.wait("scalar", sTd)
        kb.wait("tensor", sTd)
        if q4 == 0:
            kb.wait("tensor", sCast, cast_w)
        if q4 == 0:
            kb.v("memset", fa(LF["gb"], 512), 0.0, inc=sDVE)
        for g in range(16):
            for coh in range(2):
                for ci in range(8):
                    kb.mm(pA[:], ba(LB["xq"] + ci * T + g * 128, 128),
                          ba(LB["wk"] + ci * D + coh * 512, 512),
                          ci == 0, ci == 7, inc=sPE if ci == 7 else None)
                for ci in range(8):
                    kb.mm(pB[:], ba(LB["xq"] + ci * T + g * 128, 128),
                          ba(LB["wv"] + ci * D + coh * 512, 512),
                          ci == 0, ci == 7, inc=sPE if ci == 7 else None)
                kb.wait("vector", sPE)
                s1 = fa(LF["s1"], 512)
                s2 = fa(LF["s2"], 512)
                kb.stt(s1, pA[:], 1.0, fa(LF["bbc"] + coh * 512, 512),
                       ALU.mult, ALU.add, inc=sDVE)
                kb.stt(s2, pB[:], 1.0, fa(LF["bbc"] + D + coh * 512, 512),
                       ALU.mult, ALU.add, inc=sDVE)
                kb.wait("tensor", sDVE)
                kb.v("tensor_copy",
                     ba(LB["vb"] + (g % 4) * D + coh * 512, 512), s2, inc=sDVE)
                kb.wait("scalar", sDVE)
                kb.act(fa(LF["sq"], 512), s1, AF.Relu, inc=sACT)
                kb.wait("vector", sACT)
                kb.v("tensor_scalar_min", s1, s1, 0.0, inc=sDVE)
                kb.wait("scalar", sDVE)
                kb.act(s1, s1, AF.Exp, inc=sACT)
                kb.wait("vector", sACT)
                kb.v("tensor_add", s1, s1, fa(LF["sq"], 512), inc=sDVE)
                kb.v("tensor_copy",
                     ba(LB["k1"] + (g % 4) * D + coh * 512, 512), s1, inc=sDVE)
            if g % 4 == 3:
                kb.wait("tensor", sDVE)
                for hp in range(8):
                    for hh in range(2):
                        for t4 in range(4):
                            h = hp * 2 + hh
                            kb.mm(pKV[hh * 64:hh * 64 + 64, 0:64],
                                  ba(LB["k1"] + t4 * D + h * 64, 64),
                                  ba(LB["vb"] + t4 * D + h * 64, 64),
                                  t4 == 0, t4 == 3,
                                  inc=sPE if (t4 == 3 and hh == 1) else None)
                    kb.wait("vector", sPE)
                    kb.op("vector", lambda e, o=FA[:, LF["gb"] + hp * 64:LF["gb"] + hp * 64 + 64],
                          a=FA[:, LF["gb"] + hp * 64:LF["gb"] + hp * 64 + 64],
                          b=pKV[:, 0:64]: e.tensor_add(o, a, b),
                          waits=[(sDVE, kb.val(sDVE))], inc=sDVE)
                    kb.wait("tensor", sDVE)
    kv_done = kb.val(sPE)
    kb.wait("vector", sPE, kv_done)
    kb.v("memset", kvbd[:], 0.0, inc=sDVE)
    for h in range(NH):
        prow = (h % 2) * 64
        hp = h // 2
        dro = (h % 2) * 64
        kb.v("tensor_copy",
             kvbd[dro:dro + 64, hp * 128 + dro:hp * 128 + dro + 64],
             FA[prow:prow + 64, LF["gb"] + hp * 64:LF["gb"] + hp * 64 + 64],
             inc=sDVE)

    LQ = lay(BA_COLS, xq=8 * T, wq=8 * D, q1=8 * T)
    for ci in range(8):
        w = [(sPE, kv_done)] if ci == 0 else []
        kb.dma("gpsimd", ba(LQ["wq"] + ci * D, D),
               wq[ci * 128:(ci + 1) * 128, :], sCast, waits=w)
    kb.wait("gpsimd", sCast)
    for ci in range(8):
        w = [(sPE, kv_done)] if ci == 0 else []
        kb.dmat("scalar", ba(LQ["xq"] + ci * T, T),
                xkbf[0:T, ci * 128:(ci + 1) * 128], sTd, waits=w)
    kb.wait("scalar", sTd)
    kb.wait("tensor", sTd)
    kb.wait("tensor", sCast)
    for co in range(8):
        for tc in range(4):
            for ci in range(8):
                kb.mm(pA[:], ba(LQ["wq"] + ci * D + co * 128, 128),
                      ba(LQ["xq"] + ci * T + tc * 512, 512),
                      ci == 0, ci == 7, inc=sPE if ci == 7 else None)
            kb.wait("scalar", sPE)
            kb.wait("scalar", sDVE)
            kb.act(fa(LF["s1"], 512), pA[:], AF.Identity,
                   bias=bq_cols[:, co:co + 1], inc=sACT)
            kb.act(fa(LF["s2"], 512), pA[:], AF.Relu,
                   bias=bq_cols[:, co:co + 1], inc=sACT)
            kb.wait("vector", sACT)
            kb.v("tensor_scalar_min", fa(LF["s1"], 512), fa(LF["s1"], 512),
                 0.0, inc=sDVE)
            kb.wait("scalar", sDVE)
            kb.act(fa(LF["s1"], 512), fa(LF["s1"], 512), AF.Exp, inc=sACT)
            kb.wait("vector", sACT)
            kb.v("tensor_add", fa(LF["s1"], 512), fa(LF["s1"], 512),
                 fa(LF["s2"], 512), inc=sDVE)
            kb.v("tensor_copy", ba(LQ["q1"] + co * T + tc * 512, 512),
                 fa(LF["s1"], 512), inc=sDVE)
            kb.wait("tensor", sDVE)

    for ci in range(8):
        w = [(sPE, kb.val(sPE))] if ci == 0 else []
        kb.dma("gpsimd", ba(LQ["wq"] + ci * D, D),
               wp[ci * 128:(ci + 1) * 128, :], sCast, waits=w)
    kb.wait("gpsimd", sCast)
    kb.wait("sync", sLd)
    for i in range(16):
        kb.dma("sync", fa(LF["xr"] + i * D, D), xk[i * 128:(i + 1) * 128, :],
               sLd)
    kb.wait("sync", sLd)
    for hp in range(8):
        for tc in range(4):
            kb.mm(pA[:], kvbd[:, hp * 128:(hp + 1) * 128],
                  ba(LQ["q1"] + hp * T + tc * 512, 512), True, True, inc=sPE)
            kb.wait("vector", sPE)
            kb.v("tensor_copy", ba(LQ["q1"] + hp * T + tc * 512, 512), pA[:],
                 inc=sDVE)
            kb.wait("tensor", sDVE)
    kb.wait("tensor", sCast)
    kb.wait("vector", sLd)
    for tt in range(16):
        for coh in range(2):
            for ci in range(8):
                kb.mm(pC[:, coh * 512:(coh + 1) * 512],
                      ba(LQ["q1"] + ci * T + tt * 128, 128),
                      ba(LQ["wq"] + ci * D + coh * 512, 512),
                      ci == 0, ci == 7, inc=sPE if ci == 7 else None)
        kb.wait("vector", sPE)
        for coh in range(2):
            xo = LF["xr"] + tt * D + coh * 512
            kb.stt(fa(LF["s1"], 512), pC[:, coh * 512:(coh + 1) * 512], 1.0,
                   fa(LF["bbc"] + 2 * D + coh * 512, 512), ALU.mult, ALU.add,
                   inc=sDVE)
            kb.v("tensor_add", fa(xo, 512), fa(xo, 512), fa(LF["s1"], 512),
                 inc=sDVE)
        kb.wait("tensor", sDVE)
    emit_layernorm_tm(kb, lambda i: fa(LF["xr"] + i * D, D),
                      fa(LF["sq"], D), cols, 16, D, sems)

    LC = lay(BA_COLS, xq=8 * T, wq=8 * D, x1cb=8 * T, stage=D)
    assert LC["x1cb"] == LQ["q1"]
    for i in range(16):
        kb.wait("vector", sACT)
        kb.v("tensor_copy", ba(LC["stage"], D), fa(LF["xr"] + i * D, D),
             inc=sDVE)
        kb.wait("tensor", sDVE)
        for ci in range(8):
            kb.trans(pT[:], ba(LC["stage"] + ci * 128, 128), idb[:], inc=sPE)
            kb.wait("vector", sPE)
            kb.v("tensor_copy", ba(LC["x1cb"] + ci * T + i * 128, 128), pT[:],
                 inc=sDVE)
            kb.wait("tensor", sDVE)

    if moe:
        kb.wait("gpsimd", sDVE)
        kb.dma("gpsimd", x1hb, BA[:, LC["x1cb"]:LC["x1cb"] + 8 * T], sCast)
        kb.wait("gpsimd", sCast)
        build_moe_l1(kb, (gwT, eb_r, w1h, w2h, w3h, x1hb, gemh),
                     (FA, BA, LF, LC), (pA, pB, pC, pR, pTF, pKV),
                     (sLd, sCast, sTd, sPE, sDVE, sACT), (ones1, idf, cols))
        kb.fence()
        emit_layernorm_tm(kb, lambda i: fa(LF["xr"] + i * D, D),
                          fa(LF["sq"], D), cols, 16, D, sems)

    for i in range(16):
        kb.wait("vector", sACT)
        if i >= 1:
            kb.wait("vector", sSt, 32 * i)
        kb.v("tensor_copy", ba(LC["stage"], D), fa(LF["xr"] + i * D, D),
             inc=sDVE)
        kb.wait("gpsimd", sDVE)
        kb.dma("gpsimd", xout[i * 128:(i + 1) * 128, :],
               fa(LF["xr"] + i * D, D), sSt)
        kb.dma("gpsimd", xout_bf[i * 128:(i + 1) * 128, :],
               ba(LC["stage"], D), sSt)
        kb.wait("gpsimd", sSt)
    kb.op("gpsimd", lambda e: e.memset(cols[0:1, 0:1], 0.0),
          waits=[(sSt, kb.val(sSt))])

    kb.emit()
    return kb


def build_moe_l1(kb, drams, mem, psums, sems6, consts):
    gwT, eb_r, w1h, w2h, w3h, x1hb, gemh = drams
    FA, BA, LF, LC = mem
    pA, pB, pC, pR, pTF, pKV = psums
    sLd, sCast, sTd, sPE, sDVE, sACT = sems6
    ones1, idf, cols = consts
    fa = lambda o, n: FA[:, o:o + n]
    ba = lambda o, n: BA[:, o:o + n]
    THIRDS = ((0, 8), (8, 15), (15, 22))

    GS = LF["gsm"]
    gcol = FA[:, GS:GS + 8]
    key = FA[:, GS + 8:GS + 16]
    msk = FA[:, GS + 16:GS + 24]
    ebb = FA[:, GS + 24:GS + 32]
    neg = FA[:, GS + 32:GS + 40]
    gtm = FA[:, GS + 40:GS + 48]
    ebr = FA[0:1, GS + 48:GS + 56]

    gw_f = FA[:, LF["gsm"] + 64:LF["gsm"] + 64 + 64]
    kb.wait("sync", sLd)
    kb.dma("sync", ebr, eb_r, sLd)
    for ci in range(8):
        kb.dma("sync", FA[:, LF["gsm"] + 64 + ci * NE:LF["gsm"] + 64 + (ci + 1) * NE],
               gwT[ci * 128:(ci + 1) * 128, :], sLd)
    kb.wait("sync", sLd)
    kb.v("memset", neg[:], -1e30, inc=sDVE)
    kb.wait("tensor", sLd)
    kb.wait("tensor", sDVE)
    kb.mm(pR[:, 0:8], ones1[:, :], ebr, True, True, inc=sPE)
    kb.wait("vector", sPE)
    kb.v("tensor_copy", ebb[:], pR[:, 0:8], inc=sDVE)
    kb.wait("tensor", sDVE)
    for tt in range(16):
        for ci in range(8):
            kb.trans(pTF[:], FA[:, LF["xr"] + tt * D + ci * 128:LF["xr"] + tt * D + ci * 128 + 128],
                     idf[:], inc=sPE)
            kb.wait("vector", sPE)
            kb.op("vector", lambda e, o=FA[:, LF["x1cf"] + ci * 128:LF["x1cf"] + (ci + 1) * 128],
                  p=pTF[:]: e.tensor_copy(o, p), inc=sDVE)
            kb.wait("tensor", sDVE)
        for ci in range(8):
            kb.mm(pR[:, 0:8],
                  FA[:, LF["x1cf"] + ci * 128:LF["x1cf"] + (ci + 1) * 128],
                  gw_f[:, ci * NE:(ci + 1) * NE],
                  ci == 0, ci == 7, inc=sPE if ci == 7 else None)
        kb.wait("vector", sPE)
        kb.op("vector", lambda e, o=gcol[:, 0:1], p=pR[:, 0:8]:
              e.reduce_max(o, p, AX.X), inc=sDVE)
        kb.v("tensor_scalar_mul", gcol[:, 1:2], gcol[:, 0:1], -1.0, inc=sDVE)
        kb.wait("scalar", sDVE)
        kb.op("scalar", lambda e, o=gtm[:], p=pR[:, 0:8], b=gcol[:, 1:2]:
              e.activation(o, p, AF.Exp, bias=b), inc=sACT)
        kb.wait("vector", sACT)
        kb.wait("tensor", sACT)
        kb.op("vector", lambda e, o=gcol[:, 2:3], i=gtm[:]:
              e.reduce_sum(o, i, AX.X), inc=sDVE)
        kb.v("reciprocal", gcol[:, 3:4], gcol[:, 2:3], inc=sDVE)
        kb.op("vector", lambda e, o=gtm[:], i=gtm[:], m=gcol[:, 3:4]:
              e.tensor_scalar_mul(o, i, m), inc=sDVE)
        kb.v("tensor_add", key[:], gtm[:], ebb[:], inc=sDVE)
        kb.op("vector", lambda e, o=gcol[:, 4:5], i=key[:]:
              e.reduce_max(o, i, AX.X), inc=sDVE)
        kb.op("vector", lambda e, o=msk[:], i=key[:], m=gcol[:, 4:5]:
              e.tensor_single_scalar(o, i, m, ALU.is_equal), inc=sDVE)
        kb.v("tensor_scalar_mul", msk[:], msk[:], -1e30, inc=sDVE)
        kb.v("tensor_add", msk[:], msk[:], key[:], inc=sDVE)
        kb.op("vector", lambda e, o=gcol[:, 5:6], i=msk[:]:
              e.reduce_max(o, i, AX.X), inc=sDVE)
        kb.op("vector", lambda e, o=key[:], i=key[:], m=gcol[:, 5:6]:
              e.tensor_single_scalar(o, i, m, ALU.is_ge), inc=sDVE)
        kb.v("tensor_mul", gtm[:], gtm[:], key[:], inc=sDVE)
        kb.wait("tensor", sDVE)
        kb.trans(pTF[0:8, :], gtm[:], idf[:], inc=sPE)
        kb.wait("vector", sPE)
        kb.op("vector",
              lambda e, o=FA[0:8, LF["gem"] + tt * 128:LF["gem"] + (tt + 1) * 128],
              p=pTF[0:8, :]: e.tensor_copy(o, p), inc=sDVE)
        kb.wait("tensor", sDVE)
    kb.wait("gpsimd", sDVE)
    kb.dma("gpsimd", gemh, FA[0:8, LF["gem"]:LF["gem"] + T], sCast)
    kb.wait("gpsimd", sCast)

    LM = lay(BA_COLS, xe=8 * T, wA=12288, wB=12288, hsb=8 * 512)
    SIX = (0, 4, 8, 12, 16, 19, 22)
    gemv = kb.val(sCast)

    def prep_xe(e, pe_guard):
        kb.wait("gpsimd", sPE, pe_guard)
        kb.dma("gpsimd", BA[:, LM["xe"]:LM["xe"] + 8 * T], x1hb, sCast)
        kb.wait("gpsimd", sCast)
        kb.wait("sync", sLd)
        kb.wait("sync", sPE, pe_guard)
        kb.dma("sync", FA[0:1, LF["gem"]:LF["gem"] + T], gemh[e:e + 1, :],
               sLd, waits=[(sCast, gemv)])
        kb.wait("tensor", sLd)
        for q in range(16):
            kb.mm(pTF[:], ones1[:, :],
                  FA[0:1, LF["gem"] + q * 128:LF["gem"] + (q + 1) * 128],
                  True, True, inc=sPE)
            kb.wait("scalar", sPE)
            kb.act(fa(LF["gb"] + q * 128, 128), pTF[:], AF.Identity, inc=sACT)
            kb.wait("tensor", sACT)
        kb.wait("vector", sCast)
        kb.wait("vector", sACT)
        for ci in range(8):
            for tc in range(4):
                sl = LM["xe"] + ci * T + tc * 512
                kb.op("vector", lambda e_, o=ba(sl, 512), a=ba(sl, 512),
                      b=fa(LF["gb"] + tc * 512, 512):
                      e_.tensor_mul(o, a, b), inc=sDVE)
        return kb.val(sDVE)

    def load_sixth(gi, guard):
        e, s6 = divmod(gi, 6)
        nb = SIX[s6 + 1] - SIX[s6]
        hc0 = SIX[s6]
        woff = (LM["wA"], LM["wB"])[gi % 2]
        kb.wait("gpsimd", sPE, guard)
        for ci in range(8):
            kb.dma("gpsimd", ba(woff + ci * 512, nb * 128),
                   w1h[e, ci * 128:(ci + 1) * 128,
                       hc0 * 128:(hc0 + nb) * 128], sCast)
            kb.dma("gpsimd", ba(woff + 4096 + ci * 512, nb * 128),
                   w3h[e, ci * 128:(ci + 1) * 128,
                       hc0 * 128:(hc0 + nb) * 128], sCast)
        for hl in range(nb):
            kb.dma("gpsimd", ba(woff + 8192 + hl * 1024, 1024),
                   w2h[e, (hc0 + hl) * 128:(hc0 + hl) * 128 + 128, :], sCast)
        kb.wait("gpsimd", sCast)
        wload[gi] = kb.val(sCast)

    xe_ready = prep_xe(0, kb.val(sPE))
    last_mm = {}
    wload = {}
    load_sixth(0, kb.val(sPE))
    for e in range(NE):
        for s6 in range(6):
            gi = e * 6 + s6
            nb = SIX[s6 + 1] - SIX[s6]
            woff = (LM["wA"], LM["wB"])[gi % 2]
            if gi + 1 < 48:
                load_sixth(gi + 1, last_mm.get(gi - 1, 0))
            kb.wait("tensor", sCast, wload[gi])
            if s6 == 0:
                kb.wait("tensor", sDVE, xe_ready)
            out_done_tc = {}
            multd = {}
            for tc in range(4):
                hpar = LM["hsb"] + (tc % 2) * 4 * 512
                for hl in range(nb):
                    p1, p3 = ((pA, pB), (pKV, pR))[hl % 2]
                    for pt, wof in ((p1, woff), (p3, woff + 4096)):
                        for ci in range(8):
                            w = []
                            if ci == 0 and pt is p1 and hl >= 2:
                                w = [(sDVE, multd[hl - 2])]
                            kb.mm(pt[:],
                                  ba(wof + ci * 512 + hl * 128, 128),
                                  ba(LM["xe"] + ci * T + tc * 512, 512),
                                  ci == 0, ci == 7, waits=w,
                                  inc=sPE if ci == 7 else None)
                    kb.wait("scalar", sPE)
                    kb.wait("scalar", sDVE)
                    kb.act(fa(LF["s1"], 512), p1[:], AF.Silu, inc=sACT)
                    kb.wait("vector", sACT)
                    if tc >= 2:
                        kb.wait("vector", sPE, out_done_tc.get(tc - 2, 0))
                    kb.op("vector", lambda e_, o=ba(hpar + hl * 512, 512),
                          a=fa(LF["s1"], 512), b=p3[:]:
                          e_.tensor_mul(o, a, b), inc=sDVE)
                    multd[hl] = kb.val(sDVE)
                mult_done = kb.val(sDVE)
                for ttl in range(4):
                    tt = tc * 4 + ttl
                    for coh in range(2):
                        for hl in range(nb):
                            w = []
                            if ttl == 0 and coh == 0 and hl == 0:
                                w = [(sDVE, mult_done)]
                            kb.mm(pC[:, coh * 512:(coh + 1) * 512],
                                  ba(hpar + hl * 512 + ttl * 128, 128),
                                  ba(woff + 8192 + hl * 1024 + coh * 512, 512),
                                  hl == 0, hl == nb - 1, waits=w,
                                  inc=sPE if (hl == nb - 1 and coh == 1) else None)
                    kb.wait("vector", sPE)
                    for coh in range(2):
                        xo = LF["xr"] + tt * D + coh * 512
                        kb.op("vector", lambda e_, o=fa(xo, 512),
                              a=fa(xo, 512),
                              b=pC[:, coh * 512:(coh + 1) * 512]:
                              e_.tensor_add(o, a, b), inc=sDVE)
                    kb.wait("tensor", sDVE)
                out_done_tc[tc] = kb.val(sPE)
            last_mm[gi] = kb.val(sPE)
            if e < NE - 1 and s6 == 5:
                xe_ready = prep_xe(e + 1, last_mm[gi])



def build_l2a():
    kb = KB()
    dram = lambda n, s, dt, k: kb.nc.dram_tensor(n, s, dt, kind=k).ap()
    x2c = dram("x2c", [D, TB], BF16, "ExternalInput")
    winx = dram("winx", [D, 512], F32, "ExternalInput")
    winz = dram("winz", [D, 512], F32, "ExternalInput")
    cw = dram("cw", [128, 16], F32, "ExternalInput")
    cb = dram("cb", [128, 4], F32, "ExternalInput")
    u_b = dram("u_b", [512, TB], BF16, "ExternalOutput")
    u_f = dram("u_f", [512, TB], F32, "ExternalOutput")
    z_f = dram("z_f", [512, TB], F32, "ExternalOutput")

    sLd = kb.sem("sLd"); sCast = kb.sem("sCast"); sPE = kb.sem("sPE")
    sDVE = kb.sem("sDVE"); sACT = kb.sem("sACT"); sSt = kb.sem("sSt")

    cw_sb = kb.sbuf("cw_sb", [128, 16], F32)
    cb_sb = kb.sbuf("cb_sb", [128, 4], F32)
    wx_sb = kb.sbuf("wx_sb", [128, 8 * 512], BF16)
    wz_sb = kb.sbuf("wz_sb", [128, 8 * 512], BF16)
    xc = kb.sbuf("xc", [128, 2 * 8 * T], BF16)
    xs = kb.sbuf("xs", [128, 4 * (T + 3)], F32)
    ua = kb.sbuf("ua", [128, T], F32)
    us = kb.sbuf("us", [128, T], F32)
    ub = kb.sbuf("ub", [128, T], BF16)
    zs = kb.sbuf("zs", [128, T], F32)
    pA = kb.psum([128, 512]); pB = kb.psum([128, 512])

    kb.dma("sync", cw_sb[:], cw, sLd)
    kb.dma("sync", cb_sb[:], cb, sLd)
    kb.wait("sync", sLd)
    for ci in range(8):
        kb.dma("gpsimd", wx_sb[:, ci * 512:(ci + 1) * 512],
               winx[ci * 128:(ci + 1) * 128, :], sCast)
        kb.dma("gpsimd", wz_sb[:, ci * 512:(ci + 1) * 512],
               winz[ci * 128:(ci + 1) * 128, :], sCast)
    kb.wait("gpsimd", sCast)
    for dt in range(4):
        kb.v("memset", xs[:, dt * (T + 3):dt * (T + 3) + 3], 0.0, inc=sDVE)

    for ch in range(4):
        par = ch % 2
        xcb = lambda ci: xc[:, par * 8 * T + ci * T:par * 8 * T + (ci + 1) * T]
        w = [(sPE, kb.val(sPE))] if ch >= 2 else []
        kb.wait("sync", sLd)
        for ci in range(8):
            kb.dma("sync", xcb(ci), x2c[ci * 128:(ci + 1) * 128,
                                        ch * T:(ch + 1) * T], sLd,
                   waits=w if ci == 0 else ())
        kb.wait("sync", sLd)
        kb.wait("tensor", sLd)
        if ch == 0:
            kb.wait("tensor", sCast, kb.val(sCast))
        for dt in range(4):
            xso = dt * (T + 3)
            for t4 in range(4):
                for ci in range(8):
                    kb.mm(pA[:], wx_sb[:, ci * 512 + dt * 128:ci * 512 + dt * 128 + 128],
                          xcb(ci)[:, t4 * 512:(t4 + 1) * 512],
                          ci == 0, ci == 7, inc=sPE if ci == 7 else None)
                for ci in range(8):
                    kb.mm(pB[:], wz_sb[:, ci * 512 + dt * 128:ci * 512 + dt * 128 + 128],
                          xcb(ci)[:, t4 * 512:(t4 + 1) * 512],
                          ci == 0, ci == 7, inc=sPE if ci == 7 else None)
                kb.wait("vector", sPE)
                kb.v("tensor_copy", xs[:, xso + 3 + t4 * 512:xso + 3 + (t4 + 1) * 512],
                     pA[:], inc=sDVE)
                kb.v("tensor_copy", zs[:, t4 * 512:(t4 + 1) * 512], pB[:],
                     inc=sDVE)
                kb.wait("tensor", sDVE)
            kb.wait("gpsimd", sDVE)
            kb.wait("gpsimd", sSt)
            kb.dma("gpsimd", z_f[dt * 128:(dt + 1) * 128, ch * T:(ch + 1) * T],
                   zs[:], sSt)
            kb.op("vector", lambda e, o=ua[:], i=xs[:, xso:xso + T],
                  s=cw_sb[:, dt * 4:dt * 4 + 1]:
                  e.tensor_scalar_mul(o, i, s), inc=sDVE)
            for j in range(1, 4):
                kb.stt(ua[:], xs[:, xso + j:xso + j + T],
                       cw_sb[:, dt * 4 + j:dt * 4 + j + 1], ua[:],
                       ALU.mult, ALU.add, inc=sDVE)
            kb.v("tensor_copy", xs[:, xso:xso + 3],
                 xs[:, xso + T:xso + T + 3], inc=sDVE)
            kb.wait("scalar", sDVE)
            kb.wait("scalar", sSt, kb.val(sSt))
            kb.act(us[:], ua[:], AF.Silu, bias=cb_sb[:, dt:dt + 1], inc=sACT)
            kb.wait("vector", sACT)
            kb.v("tensor_copy", ub[:], us[:], inc=sDVE)
            kb.wait("gpsimd", sDVE)
            kb.wait("gpsimd", sSt)
            kb.dma("gpsimd", u_f[dt * 128:(dt + 1) * 128, ch * T:(ch + 1) * T],
                   us[:], sSt)
            kb.dma("gpsimd", u_b[dt * 128:(dt + 1) * 128, ch * T:(ch + 1) * T],
                   ub[:], sSt)
            kb.wait("vector", sSt)
    kb.op("gpsimd", lambda e: e.memset(cb_sb[0:1, 0:1], 0.0),
          waits=[(sSt, kb.val(sSt))])
    kb.emit()
    return kb



def build_l2b():
    kb = KB()
    dram = lambda n, s, dt, k: kb.nc.dram_tensor(n, s, dt, kind=k).ap()
    uf = dram("uf", [DI, TB], BF16, "ExternalInput")
    wxT = dram("wxT", [DI, 544], F32, "ExternalInput")
    din = dram("din", [512, TB], BF16, "ExternalOutput")
    bc = dram("bc", [32, TB], F32, "ExternalOutput")

    sLd = kb.sem("sLd"); sCast = kb.sem("sCast"); sPE = kb.sem("sPE")
    sDVE = kb.sem("sDVE"); sACT = kb.sem("sACT"); sSt = kb.sem("sSt")

    wx_sb = kb.sbuf("wx_sb", [128, 16 * 544], BF16)
    TC2 = 1024
    uc = kb.sbuf("uc", [128, 2 * 16 * TC2], BF16)
    st1 = kb.sbuf("st1", [128, 512], BF16)
    st2 = kb.sbuf("st2", [32, 512], F32)
    pA = kb.psum([128, 512]); pB = kb.psum([32, 512])

    for ci in range(16):
        kb.dma("gpsimd", wx_sb[:, ci * 544:(ci + 1) * 544],
               wxT[ci * 128:(ci + 1) * 128, :], sCast)
    kb.wait("gpsimd", sCast)
    for ch in range(8):
        par = ch % 2
        ucb = lambda ci: uc[:, par * 16 * TC2 + ci * TC2:par * 16 * TC2 + (ci + 1) * TC2]
        w = [(sPE, kb.val(sPE))] if ch >= 2 else []
        kb.wait("sync", sLd)
        for ci in range(16):
            kb.dma("sync", ucb(ci), uf[ci * 128:(ci + 1) * 128,
                                       ch * TC2:(ch + 1) * TC2], sLd,
                   waits=w if ci == 0 else ())
        kb.wait("sync", sLd)
        kb.wait("tensor", sLd)
        if ch == 0:
            kb.wait("tensor", sCast, kb.val(sCast))
        for t2 in range(2):
            for co in range(4):
                for ci in range(16):
                    kb.mm(pA[:], wx_sb[:, ci * 544 + co * 128:ci * 544 + co * 128 + 128],
                          ucb(ci)[:, t2 * 512:(t2 + 1) * 512],
                          ci == 0, ci == 15, inc=sPE if ci == 15 else None)
                kb.wait("vector", sPE)
                kb.wait("vector", sSt, kb.val(sSt))
                kb.v("tensor_copy", st1[:], pA[:], inc=sDVE)
                kb.wait("tensor", sDVE)
                kb.wait("gpsimd", sDVE)
                kb.wait("gpsimd", sSt)
                kb.dma("gpsimd", din[co * 128:(co + 1) * 128,
                                     ch * TC2 + t2 * 512:ch * TC2 + (t2 + 1) * 512],
                       st1[:], sSt)
            for ci in range(16):
                kb.mm(pB[:], wx_sb[:, ci * 544 + 512:ci * 544 + 544],
                      ucb(ci)[:, t2 * 512:(t2 + 1) * 512],
                      ci == 0, ci == 15, inc=sPE if ci == 15 else None)
            kb.wait("vector", sPE)
            kb.wait("vector", sSt, kb.val(sSt))
            kb.v("tensor_copy", st2[:], pB[:], inc=sDVE)
            kb.wait("tensor", sDVE)
            kb.wait("gpsimd", sDVE)
            kb.wait("gpsimd", sSt)
            kb.dma("gpsimd", bc[:, ch * TC2 + t2 * 512:ch * TC2 + (t2 + 1) * 512],
                   st2[:], sSt)
    kb.op("gpsimd", lambda e: e.memset(st2[0:1, 0:1], 0.0),
          waits=[(sSt, kb.val(sSt))])
    kb.emit()
    return kb



def build_l2c():
    kb = KB()
    dram = lambda n, s, dt, k: kb.nc.dram_tensor(n, s, dt, kind=k).ap()
    dinf = dram("dinf", [DI, TB], BF16, "ExternalInput")
    bcf = dram("bcf", [32, TB], F32, "ExternalInput")
    wdtT = dram("wdtT", [DI, 512], F32, "ExternalInput")
    bdt_c = dram("bdt_c", [128, 4], F32, "ExternalInput")
    alog_c = dram("alog_c", [128, 64], F32, "ExternalInput")
    dcol = dram("dcol", [128, 4], F32, "ExternalInput")
    uf32 = dram("uf32", [512, TB], F32, "ExternalInput")
    zf32 = dram("zf32", [512, TB], F32, "ExternalInput")
    woT = dram("woT", [512, D], F32, "ExternalInput")
    ckon_d = dram("ckon", [128, 8], F32, "ExternalInput")
    ones_d = dram("ones_d", [1, 128], F32, "ExternalInput")
    po = dram("po", [D, TB], F32, "ExternalOutput")

    sLd = kb.sem("sLd"); sCast = kb.sem("sCast"); sPE = kb.sem("sPE")
    sDVE = kb.sem("sDVE"); sACT = kb.sem("sACT"); sSt = kb.sem("sSt")
    sPool = kb.sem("sPool")

    kcon = kb.sbuf("kcon", [128, 8], F32)
    kb.dma("sync", kcon[:], ckon_d, sLd)
    for j, cv in enumerate(CONSTS):
        kb.nc.const_aps.register(F32, cv, kcon[:, j:j + 1])
    ones1 = kb.sbuf("ones1", [1, 128], F32)
    kb.dma("sync", ones1[:], ones_d, sLd)
    bdt = kb.sbuf("bdt", [128, 4], F32)
    kb.dma("sync", bdt[:], bdt_c, sLd)
    alog = kb.sbuf("alog", [128, 64], F32)
    kb.dma("sync", alog[:], alog_c, sLd)
    dco = kb.sbuf("dco", [128, 4], F32)
    kb.dma("sync", dco[:], dcol, sLd)
    kb.wait("sync", sLd)
    con_ld = kb.val(sLd)

    wdt_sb = kb.sbuf("wdt_sb", [128, 16 * 512], BF16)
    wo_sb = kb.sbuf("wo_sb", [128, 4 * D], BF16)
    for ci in range(16):
        kb.dma("gpsimd", wdt_sb[:, ci * 512:(ci + 1) * 512],
               wdtT[ci * 128:(ci + 1) * 128, :], sCast)
    for ci in range(4):
        kb.dma("gpsimd", wo_sb[:, ci * D:(ci + 1) * D],
               woT[ci * 128:(ci + 1) * 128, :], sCast)
    kb.wait("gpsimd", sCast)
    wld = kb.val(sCast)

    ac = kb.sbuf("ac", [128, 64], F32)
    kb.wait("scalar", sLd, con_ld)
    kb.act(ac[:], alog[:], AF.Exp, inc=sACT)
    kb.wait("vector", sACT)
    kb.v("tensor_scalar_mul", ac[:], ac[:], -1.0, inc=sDVE)

    TC = 1024
    dinc = kb.sbuf("dinc", [128, 16 * TC], BF16)
    brows = kb.sbuf("brows", [1, 4 * TC], F32)
    uc = kb.sbuf("ucf", [128, 4 * TC], F32)
    zc = kb.sbuf("zcf", [128, 4 * TC], F32)
    delta = kb.sbuf("delta", [128, 4 * TC], F32)
    du = kb.sbuf("du", [128, 4 * TC], F32)
    yac = kb.sbuf("yac", [128, 4 * TC], F32)
    bb = kb.sbuf("bb", [128, 2 * TC], F32)
    cbr = kb.sbuf("cbr", [128, 2 * TC], F32)
    dA = kb.sbuf("dA", [128, 2 * TC], F32)
    dBu = kb.sbuf("dBu", [128, 2 * TC], F32)
    sseq = kb.sbuf("sseq", [128, 2 * TC], F32)
    s1 = kb.sbuf("s1c", [128, 512], F32)
    s2 = kb.sbuf("s2c", [128, 512], F32)
    y2b = kb.sbuf("y2b", [128, 4 * TC], BF16)
    stc = kb.sbuf("stc", [128, 64], F32)
    pst = kb.sbuf("pst", [128, 512], F32)
    pA = kb.psum([128, 512]); pB = kb.psum([128, 512])
    pC = kb.psum([128, 512]); pD = kb.psum([128, 512])

    kb.v("memset", stc[:], 0.0, inc=sDVE)

    for ch in range(8):
        par = ch % 2
        dcb = lambda ci: dinc[:, ci * TC:(ci + 1) * TC]
        ucb = lambda dt: uc[:, dt * TC:(dt + 1) * TC]
        w = [(sPE, kb.val(sPE)), (sDVE, kb.val(sDVE))] if ch >= 1 else []
        kb.wait("sync", sLd)
        for ci in range(16):
            kb.dma("sync", dcb(ci), dinf[ci * 128:(ci + 1) * 128,
                                         ch * TC:(ch + 1) * TC], sLd,
                   waits=w if ci == 0 else ())
        for dt in range(4):
            kb.dma("sync", ucb(dt), uf32[dt * 128:(dt + 1) * 128,
                                         ch * TC:(ch + 1) * TC], sLd)
        kb.wait("sync", sLd)
        ld_ch = kb.val(sLd)
        kb.wait("tensor", sLd, ld_ch)
        if ch == 0:
            kb.wait("tensor", sCast, wld)
        for dt in range(4):
            for t2 in range(2):
                pp = (pA, pB)[t2]
                for ci in range(16):
                    kb.mm(pp[:], wdt_sb[:, ci * 512 + dt * 128:ci * 512 + dt * 128 + 128],
                          dcb(ci)[:, t2 * 512:(t2 + 1) * 512],
                          ci == 0, ci == 15, inc=sPE if ci == 15 else None)
                kb.wait("scalar", sPE)
                kb.wait("scalar", sDVE)
                do = dt * TC + t2 * 512
                kb.act(s1[:], pp[:], AF.Abs, bias=bdt[:, dt:dt + 1], inc=sACT)
                kb.act(s1[:], s1[:], AF.Exp, scale=-1.0, inc=sACT)
                kb.act(s1[:], s1[:], AF.Ln, bias=1.0, inc=sACT)
                kb.act(s2[:], pp[:], AF.Relu, bias=bdt[:, dt:dt + 1], inc=sACT)
                kb.wait("vector", sACT)
                kb.v("tensor_add", delta[:, do:do + 512], s1[:], s2[:],
                     inc=sDVE)
                kb.wait("tensor", sDVE)
            kb.wait("vector", sLd, ld_ch)
            kb.v("tensor_mul", du[:, dt * TC:(dt + 1) * TC],
                 delta[:, dt * TC:(dt + 1) * TC], ucb(dt), inc=sDVE)
        kb.v("memset", yac[:], 0.0, inc=sDVE)
        yz = kb.val(sDVE)
        dA_done = {}
        dBu_done = {}
        tts_done = {}
        pool_done = {}
        bc_done = {}
        for n in range(DS):
            np_ = n % 2
            bro = np_ * 2 * TC
            w = [(sDVE, dBu_done[(n - 2) * 4 + 3])] if n >= 2 else []
            kb.wait("sync", sLd)
            kb.dma("sync", brows[0:1, bro:bro + TC],
                   bcf[n:n + 1, ch * TC:(ch + 1) * TC], sLd, waits=w)
            kb.dma("sync", brows[0:1, bro + TC:bro + 2 * TC],
                   bcf[16 + n:17 + n, ch * TC:(ch + 1) * TC], sLd)
            kb.wait("sync", sLd)
            kb.wait("tensor", sLd)
            for hseg in range(4):
                kb.mm(pC[:], ones1[:, :],
                      brows[0:1, bro + hseg * 512:bro + hseg * 512 + 512],
                      True, True, inc=sPE)
                kb.wait("scalar", sPE)
                if n >= 2:
                    kb.wait("scalar", sDVE, tts_done[(n - 2) * 4 + 3])
                dst = (bb if hseg < 2 else cbr)
                off = np_ * TC + (hseg % 2) * 512
                kb.act(dst[:, off:off + 512], pC[:], AF.Identity, inc=sACT)
                kb.wait("tensor", sACT)
            bc_done[n] = kb.val(sACT)
            for dt in range(4):
                it = n * 4 + dt
                p2 = it % 2
                sl = slice(dt * TC, (dt + 1) * TC)
                if it >= 2:
                    kb.wait("scalar", sDVE, tts_done[it - 2])
                kb.act(dA[:, p2 * TC:(p2 + 1) * TC], delta[:, sl], AF.Exp,
                       scale=ac[:, dt * 16 + n:dt * 16 + n + 1], inc=sACT)
                dA_done[it] = kb.val(sACT)
                if it >= 2:
                    kb.wait("vector", sDVE, tts_done[it - 2])
                kb.wait("vector", sACT, bc_done[n])
                kb.v("tensor_mul", dBu[:, p2 * TC:(p2 + 1) * TC], du[:, sl],
                     bb[:, np_ * TC:(np_ + 1) * TC], inc=sDVE)
                dBu_done[it] = kb.val(sDVE)
                kb.wait("vector", sACT, dA_done[it])
                if it >= 2:
                    kb.wait("vector", sPool, pool_done[it - 2])
                kb.op("vector", lambda e, o=sseq[:, p2 * TC:(p2 + 1) * TC],
                      a=dA[:, p2 * TC:(p2 + 1) * TC],
                      b=dBu[:, p2 * TC:(p2 + 1) * TC],
                      i=stc[:, dt * 16 + n:dt * 16 + n + 1]:
                      e.tensor_tensor_scan(o, a, b, i, ALU.mult, ALU.add),
                      inc=sDVE)
                kb.v("tensor_copy", stc[:, dt * 16 + n:dt * 16 + n + 1],
                     sseq[:, p2 * TC + TC - 1:p2 * TC + TC], inc=sDVE)
                kb.op("vector", lambda e, o=sseq[:, p2 * TC:(p2 + 1) * TC],
                      a=sseq[:, p2 * TC:(p2 + 1) * TC],
                      b=cbr[:, np_ * TC:(np_ + 1) * TC]:
                      e.tensor_mul(o, a, b), inc=sDVE)
                tts_done[it] = kb.val(sDVE)
                kb.wait("gpsimd", sDVE, tts_done[it])
                if it < 4:
                    kb.wait("gpsimd", sDVE, yz)
                if it >= 4:
                    kb.wait("gpsimd", sPool, pool_done[it - 4])
                kb.op("gpsimd", lambda e, o=yac[:, sl],
                      a=yac[:, sl], b=sseq[:, p2 * TC:(p2 + 1) * TC]:
                      e.tensor_tensor(o, a, b, ALU.add), inc=sPool,
                      selfwait=False)
                pool_done[it] = kb.val(sPool)
        for dt in range(4):
            kb.dma("sync", zc[:, dt * TC:(dt + 1) * TC],
                   zf32[dt * 128:(dt + 1) * 128, ch * TC:(ch + 1) * TC], sLd)
        kb.wait("sync", sLd)
        kb.wait("scalar", sLd)
        kb.wait("scalar", sDVE)
        kb.wait("vector", sPool)
        kb.wait("scalar", sPool)
        for dt in range(4):
            sl = slice(dt * TC, (dt + 1) * TC)
            kb.stt(yac[:, sl], ucb(dt), dco[:, dt:dt + 1], yac[:, sl],
                   ALU.mult, ALU.add, inc=sDVE)
            kb.wait("scalar", sDVE)
            kb.act(zc[:, sl], zc[:, sl], AF.Silu, inc=sACT)
            kb.wait("vector", sACT)
            kb.v("tensor_mul", yac[:, sl], yac[:, sl], zc[:, sl], inc=sDVE)
            kb.v("tensor_copy", y2b[:, sl], yac[:, sl], inc=sDVE)
        kb.wait("tensor", sDVE)
        for co in range(8):
            for t2 in range(2):
                pp = (pC, pD)[t2]
                for ci in range(4):
                    kb.mm(pp[:], wo_sb[:, ci * D + co * 128:ci * D + co * 128 + 128],
                          y2b[:, ci * TC + t2 * 512:ci * TC + t2 * 512 + 512],
                          ci == 0, ci == 3, inc=sPE if ci == 3 else None)
                kb.wait("vector", sPE)
                kb.wait("vector", sSt, kb.val(sSt))
                kb.v("tensor_copy", pst[:], pp[:], inc=sDVE)
                kb.wait("tensor", sDVE)
                kb.wait("gpsimd", sDVE)
                kb.wait("gpsimd", sSt)
                kb.dma("gpsimd", po[co * 128:(co + 1) * 128,
                                    ch * TC + t2 * 512:ch * TC + t2 * 512 + 512],
                       pst[:], sSt)
    kb.op("gpsimd", lambda e: e.memset(s1[0:1, 0:1], 0.0),
          waits=[(sSt, kb.val(sSt))])
    kb.emit()
    return kb



L3_FA = 9280
L3_BA = 77824


def build_l3():
    kb = KB()
    dram = lambda n, s, dt, k: kb.nc.dram_tensor(n, s, dt, kind=k).ap()
    p0 = dram("p0", [T, D], F32, "ExternalInput")
    p1 = dram("p1", [T, D], F32, "ExternalInput")
    p2 = dram("p2", [T, D], F32, "ExternalInput")
    p3 = dram("p3", [T, D], F32, "ExternalInput")
    x2 = dram("x2", [T, D], F32, "ExternalInput")
    lng = dram("lng", [1, D], F32, "ExternalInput")
    lnb = dram("lnb", [1, D], F32, "ExternalInput")
    k0t = dram("k0t", [D, 4096], U8, "ExternalInput")
    k0b = dram("k0b", [D, 4096], U8, "ExternalInput")
    k1 = dram("k1", [4096, 4096], U8, "ExternalInput")
    k2 = dram("k2", [4096, D], U8, "ExternalInput")
    scol = dram("scol", [128, 3], F32, "ExternalInput")
    id128b = dram("id128b", [128, 128], BF16, "ExternalInput")
    ckon_d = dram("ckon", [128, 8], F32, "ExternalInput")
    ones_d = dram("ones_d", [1, 128], F32, "ExternalInput")
    xo = dram("xo", [T, D], F32, "ExternalOutput")
    w0dh = dram("w0dh", [D, 4096], BF16, "Internal")
    w1dh = dram("w1dh", [4096, 4096], BF16, "Internal")
    w2dh = dram("w2dh", [4096, D], BF16, "Internal")
    x3fh = dram("x3fh", [T, D], F32, "Internal")
    x3bh = dram("x3bh", [D, T], BF16, "Internal")

    sLd = kb.sem("sLd")
    kb.sLd_h = sLd
    sPE = kb.sem("sPE")
    sDVE = kb.sem("sDVE")
    sACT = kb.sem("sACT")
    sSt = kb.sem("sSt")
    sWb0 = kb.sem("sWb0")
    sWb1 = kb.sem("sWb1")
    sems = (sDVE, sACT)

    kcon = kb.sbuf("kcon", [128, 8], F32)
    kb.dma("sync", kcon[:], ckon_d, sLd)
    for j, cv in enumerate(CONSTS):
        kb.nc.const_aps.register(F32, cv, kcon[:, j:j + 1])
    ones1 = kb.sbuf("ones1", [1, 128], F32)
    kb.dma("sync", ones1[:], ones_d, sLd)
    idb = kb.sbuf("idb", [128, 128], BF16)
    kb.dma("sync", idb[:], id128b, sLd)
    sc_sb = kb.sbuf("sc_sb", [128, 4], F32)
    kb.dma("sync", sc_sb[:, 0:3], scol, sLd)
    cols = kb.sbuf("cols", [128, 8], F32)
    gbc = kb.sbuf("gbc", [128, 2 * D], F32)

    FA = kb.sbuf("FA", [128, L3_FA], F32)
    BA = kb.sbuf("BA", [128, L3_BA], BF16)
    fa = lambda o, n: FA[:, o:o + n]
    ba = lambda o, n: BA[:, o:o + n]
    LF = lay(L3_FA, pl=5 * D, sq=D, s1=512, s2=512, xt=D)
    lr = FA[0:1, LF["pl"]:LF["pl"] + 2 * D]
    kb.dma("sync", lr[:, 0:D], lng, sLd)
    kb.dma("sync", lr[:, D:2 * D], lnb, sLd)
    kb.wait("sync", sLd)
    kb.kcon_ld = kb.val(sLd)
    LB = lay(L3_BA, h1=32 * 1024, h2=32 * 1024, x3h=8 * 1024, wst=4096)
    WB = LB["x3h"]
    pF = kb.psum([128, 3584])
    pT = kb.psum([128, 128], BF16)

    kb.wait("vector", sLd, kb.kcon_ld)
    kb.v("tensor_scalar_mul", sc_sb[:, 0:2], sc_sb[:, 0:2], 2.0, inc=sDVE)
    kb.wait("tensor", sLd, kb.kcon_ld)
    kb.wait("tensor", sDVE)
    for j in range(4):
        kb.mm(pF[:, 0:512], ones1[:, :], lr[:, j * 512:(j + 1) * 512],
              True, True, inc=sPE)
        kb.wait("vector", sPE)
        kb.v("tensor_copy", gbc[:, j * 512:(j + 1) * 512], pF[:, 0:512],
             inc=sDVE)
        kb.wait("tensor", sDVE)

    def build_w(src_u8, dst, rows, ncols, second=None):
        ld_done = {}
        cv_done = {}
        st_done = {}
        for r in range(rows // 128):
            p = r % 2
            buf = (LB["wst"], LB["h2"])[p]
            sec = (LB["h1"], LB["h1"] + 4096)[p]
            sW = (sWb0, sWb1)[p]
            if r >= 2:
                kb.wait("gpsimd", sW, st_done[r - 2])
            kb.dma("gpsimd", ba(buf, ncols),
                   src_u8[r * 128:(r + 1) * 128, :], sSt)
            if second is not None:
                kb.dma("gpsimd", ba(sec, ncols),
                       second[r * 128:(r + 1) * 128, :], sSt)
            kb.wait("gpsimd", sSt)
            ld_done[r] = kb.val(sSt)
            kb.wait("vector", sSt, ld_done[r])
            if second is None:
                kb.op("vector", lambda e, o=ba(buf, ncols), i=ba(buf, ncols):
                      e.tensor_scalar(o, i, 2.0, -1.0, ALU.mult, ALU.add),
                      inc=sDVE)
            else:
                kb.v("tensor_sub", ba(buf, ncols), ba(buf, ncols),
                     ba(sec, ncols), inc=sDVE)
                kb.v("tensor_scalar_mul", ba(buf, ncols), ba(buf, ncols),
                     2.0, inc=sDVE)
            cv_done[r] = kb.val(sDVE)
            kb.wait("gpsimd", sDVE, cv_done[r])
            kb.dma("gpsimd", dst[r * 128:(r + 1) * 128, :], ba(buf, ncols),
                   sW)
            st_done[r] = kb.val(sW)
        kb.wait("gpsimd", sWb0)
        kb.wait("gpsimd", sWb1)
        kb.wait("vector", sWb0, kb.val(sWb0))
        kb.wait("vector", sWb1, kb.val(sWb1))
        kb.wait("sync", sWb0, kb.val(sWb0))
        kb.wait("sync", sWb1, kb.val(sWb1))

    build_w(k0t, w0dh, D, 4096, second=k0b)
    build_w(k1, w1dh, 4096, 4096)
    build_w(k2, w2dh, 4096, D)

    for i in range(16):
        kb.wait("sync", sLd)
        kb.wait("sync", sDVE)
        kb.wait("sync", sSt, kb.val(sSt))
        kb.dma("sync", fa(LF["pl"], D), p0[i * 128:(i + 1) * 128, :], sLd)
        kb.dma("sync", fa(LF["pl"] + D, D), p1[i * 128:(i + 1) * 128, :], sLd)
        kb.dma("sync", fa(LF["pl"] + 2 * D, D), p2[i * 128:(i + 1) * 128, :], sLd)
        kb.dma("sync", fa(LF["pl"] + 3 * D, D), p3[i * 128:(i + 1) * 128, :], sLd)
        kb.dma("sync", fa(LF["pl"] + 4 * D, D), x2[i * 128:(i + 1) * 128, :], sLd)
        kb.wait("sync", sLd)
        kb.wait("vector", sLd)
        m = fa(LF["pl"], D)
        kb.v("tensor_add", m, m, fa(LF["pl"] + D, D), inc=sDVE)
        kb.v("tensor_add", m, m, fa(LF["pl"] + 2 * D, D), inc=sDVE)
        kb.v("tensor_add", m, m, fa(LF["pl"] + 3 * D, D), inc=sDVE)
        kb.v("tensor_add", m, m, fa(LF["pl"] + 4 * D, D), inc=sDVE)
        emit_layernorm_tm(kb, lambda _i: m, fa(LF["sq"], D), cols, 1, D, sems)
        kb.wait("vector", sACT)
        kb.v("tensor_mul", m, m, gbc[:, 0:D], inc=sDVE)
        kb.v("tensor_add", m, m, gbc[:, D:2 * D], inc=sDVE)
        kb.v("tensor_add", m, m, fa(LF["pl"] + 4 * D, D), inc=sDVE)
        emit_layernorm_tm(kb, lambda _i: m, fa(LF["sq"], D), cols, 1, D, sems)
        kb.wait("gpsimd", sDVE)
        kb.wait("gpsimd", sACT)
        kb.wait("gpsimd", sSt)
        kb.dma("gpsimd", x3fh[i * 128:(i + 1) * 128, :], m, sSt)
        kb.wait("vector", sACT)
        kb.v("tensor_copy", ba(LB["wst"], D), m, inc=sDVE)
        kb.wait("tensor", sDVE)
        for ci in range(8):
            kb.trans(pT[:], ba(LB["wst"] + ci * 128, 128), idb[:], inc=sPE)
            kb.wait("vector", sPE)
            kb.v("tensor_copy",
                 ba(LB["x3h"] + ci * 1024 + (i % 8) * 128, 128), pT[:],
                 inc=sDVE)
            kb.wait("tensor", sDVE)
        if i % 8 == 7:
            half = i // 8
            kb.wait("gpsimd", sDVE)
            kb.wait("gpsimd", sSt)
            for ci in range(8):
                kb.dma("gpsimd", x3bh[ci * 128:(ci + 1) * 128,
                                      half * 1024:(half + 1) * 1024],
                       ba(LB["x3h"] + ci * 1024, 1024), sSt)
            kb.wait("gpsimd", sSt)
            kb.wait("vector", sSt)

    HT = 1024

    for half in range(2):
        h0 = half * HT
        kb.wait("sync", sSt, kb.val(sSt))
        kb.wait("sync", sPE)
        for ci in range(8):
            kb.dma("sync", ba(LB["x3h"] + ci * HT, HT),
                   x3bh[ci * 128:(ci + 1) * 128, h0:h0 + HT], sLd)
        kb.wait("sync", sLd)
        kb.wait("tensor", sLd)

        co = 0
        while co < 32:
            ng = min(3, 32 - co)
            kb.wait("sync", sPE)
            kb.wait("sync", sLd)
            for ci in range(8):
                kb.dma("sync", ba(LB["wst"] + ci * ng * 128, ng * 128),
                       w0dh[ci * 128:(ci + 1) * 128, co * 128:(co + ng) * 128],
                       sLd)
            kb.wait("sync", sLd)
            kb.wait("tensor", sLd)
            for ci in range(8):
                for g in range(ng):
                    for t2 in range(2):
                        kb.mm(pF[:, (g * 2 + t2) * 512:(g * 2 + t2) * 512 + 512],
                              ba(LB["wst"] + ci * ng * 128 + g * 128, 128),
                              ba(LB["x3h"] + ci * HT + t2 * 512, 512),
                              ci == 0, ci == 7,
                              inc=sPE if (ci == 7 and g == ng - 1 and t2 == 1)
                              else None)
            kb.wait("scalar", sPE)
            kb.wait("scalar", sDVE)
            for g in range(ng):
                for t2 in range(2):
                    kb.wait("scalar", sDVE)
                    kb.act(fa(LF["s1"], 512),
                           pF[:, (g * 2 + t2) * 512:(g * 2 + t2) * 512 + 512],
                           AF.Relu, scale=sc_sb[:, 0:1], inc=sACT)
                    kb.wait("vector", sACT)
                    kb.v("tensor_copy",
                         ba(LB["h1"] + (co + g) * HT + t2 * 512, 512),
                         fa(LF["s1"], 512), inc=sDVE)
            kb.wait("tensor", sDVE)
            co += ng

        co = 0
        while co < 32:
            ng = min(3, 32 - co)
            kb.wait("sync", sPE)
            kb.wait("sync", sLd)
            for ci in range(32):
                kb.dma("sync", BA[:, WB + ci * ng * 128:WB + (ci + 1) * ng * 128],
                       w1dh[ci * 128:(ci + 1) * 128, co * 128:(co + ng) * 128],
                       sLd)
            kb.wait("sync", sLd)
            kb.wait("tensor", sLd)
            for ci in range(32):
                for g in range(ng):
                    for t2 in range(2):
                        kb.mm(pF[:, (g * 2 + t2) * 512:(g * 2 + t2) * 512 + 512],
                              BA[:, WB + ci * ng * 128 + g * 128:WB + ci * ng * 128 + (g + 1) * 128],
                              ba(LB["h1"] + ci * HT + t2 * 512, 512),
                              ci == 0, ci == 31,
                              inc=sPE if (ci == 31 and g == ng - 1 and t2 == 1)
                              else None)
            kb.wait("scalar", sPE)
            kb.wait("scalar", sDVE)
            for g in range(ng):
                for t2 in range(2):
                    kb.wait("scalar", sDVE)
                    kb.act(fa(LF["s1"], 512),
                           pF[:, (g * 2 + t2) * 512:(g * 2 + t2) * 512 + 512],
                           AF.Relu, scale=sc_sb[:, 1:2], inc=sACT)
                    kb.wait("vector", sACT)
                    kb.v("tensor_copy",
                         ba(LB["h2"] + (co + g) * HT + t2 * 512, 512),
                         fa(LF["s1"], 512), inc=sDVE)
            kb.wait("tensor", sDVE)
            co += ng

        for ttg in range(0, 8, 3):
            ntt = min(3, 8 - ttg)
            for cig in range(0, 32, 11):
                nci = min(11, 32 - cig)
                kb.wait("sync", sPE)
                kb.wait("sync", sLd)
                for cl in range(nci):
                    kb.dma("sync", BA[:, WB + cl * D:WB + (cl + 1) * D],
                           w2dh[(cig + cl) * 128:(cig + cl + 1) * 128, :], sLd)
                kb.wait("sync", sLd)
                kb.wait("tensor", sLd)
                for cl in range(nci):
                    ci = cig + cl
                    for tt in range(ntt):
                        for coh in range(2):
                            kb.mm(pF[:, (tt * 2 + coh) * 512:(tt * 2 + coh) * 512 + 512],
                                  ba(LB["h2"] + ci * HT + (ttg + tt) * 128, 128),
                                  BA[:, WB + cl * D + coh * 512:WB + cl * D + coh * 512 + 512],
                                  ci == 0, ci == 31,
                                  inc=sPE if (cl == nci - 1 and tt == ntt - 1
                                              and coh == 1) else None)
            kb.wait("scalar", sPE)
            for tt in range(ntt):
                gi = half * 8 + ttg + tt
                kb.wait("sync", sDVE)
                kb.wait("sync", sSt, kb.val(sSt))
                kb.dma("sync", fa(LF["xt"], D), x3fh[gi * 128:(gi + 1) * 128, :],
                       sLd)
                kb.wait("sync", sLd)
                kb.wait("scalar", sDVE)
                for coh in range(2):
                    kb.act(fa(LF["s1"] if coh == 0 else LF["s2"], 512),
                           pF[:, (tt * 2 + coh) * 512:(tt * 2 + coh) * 512 + 512],
                           AF.Identity, scale=sc_sb[:, 2:3], inc=sACT)
                kb.wait("vector", sACT)
                kb.wait("vector", sLd)
                kb.v("tensor_add", fa(LF["xt"], 512), fa(LF["xt"], 512),
                     fa(LF["s1"], 512), inc=sDVE)
                kb.op("vector", lambda e, o=FA[:, LF["xt"] + 512:LF["xt"] + 1024],
                      a=FA[:, LF["xt"] + 512:LF["xt"] + 1024], b=fa(LF["s2"], 512):
                      e.tensor_add(o, a, b), inc=sDVE)
                emit_layernorm_tm(kb, lambda _i: fa(LF["xt"], D),
                                  fa(LF["sq"], D), cols, 1, D, sems)
                kb.wait("gpsimd", sACT)
                kb.wait("gpsimd", sSt)
                kb.dma("gpsimd", xo[gi * 128:(gi + 1) * 128, :], fa(LF["xt"], D),
                       sSt)
            kb.wait("tensor", sDVE)
            kb.wait("tensor", sACT)
    kb.op("gpsimd", lambda e: e.memset(cols[0:1, 0:1], 0.0),
          waits=[(sSt, kb.val(sSt))])
    kb.emit()
    return kb



_CACHE = {}


def _get(name, builder):
    if name not in _CACHE:
        _CACHE[name] = builder().nc
    return _CACHE[name]


def _launch(name, builder, maps):
    nc = _get(name, builder)
    return run_bass_kernel_spmd(nc, maps, core_ids=list(range(NCORES))).results


def _l1_in_maps(inputs, moe=True):
    x = np.asarray(inputs["L0_attn_x"])
    base = {
        "wq": np.ascontiguousarray(inputs["L0_attn_Wq"], np.float32),
        "wk": np.ascontiguousarray(inputs["L0_attn_Wk"], np.float32),
        "wv": np.ascontiguousarray(inputs["L0_attn_Wv"], np.float32),
        "wp": np.ascontiguousarray(inputs["L0_attn_Wp"], np.float32),
        "bq_c": np.ascontiguousarray(
            np.asarray(inputs["L0_attn_bq"], np.float32).reshape(8, 128).T),
        "bk_r": np.asarray(inputs["L0_attn_bk"], np.float32).reshape(1, D),
        "bv_r": np.asarray(inputs["L0_attn_bv"], np.float32).reshape(1, D),
        "bp_r": np.asarray(inputs["L0_attn_bp"], np.float32).reshape(1, D),
        "id128b": np.eye(128, dtype=np.float32).astype(BF),
        "idf": np.eye(128, dtype=np.float32),
        "ckon": np.tile(np.array([list(CONSTS)], np.float32), (128, 1)),
    }
    if moe:
        base.update({
            "gwT": np.ascontiguousarray(
                np.asarray(inputs["L0_ffn_gate_w"], np.float32).T),
            "eb_r": np.asarray(
                inputs["L0_ffn_expert_bias"], np.float32).reshape(1, NE),
            "w1h": np.ascontiguousarray(inputs["L0_ffn_w1"], np.float32),
            "w2h": np.ascontiguousarray(inputs["L0_ffn_w2"], np.float32),
            "w3h": np.ascontiguousarray(inputs["L0_ffn_w3"], np.float32),
        })
    maps = []
    for c in range(NCORES):
        b, off = c // 4, (c % 4) * T
        xb = np.asarray(x[b], dtype=np.float32)
        m = dict(base)
        m["xk"] = np.ascontiguousarray(
            np.concatenate([xb[off:], xb[:off]], axis=0))
        maps.append(m)
    return maps


def run_l1(inputs, moe=True):
    return _launch(("l1", moe), lambda m=moe: build_l1(moe=m),
                   _l1_in_maps(inputs, moe=moe))


def _col4(v):
    return np.ascontiguousarray(np.asarray(v, np.float32).reshape(4, 128).T)


def _l2a_in_maps(inputs, x2b_full):
    w_in = np.asarray(inputs["L1_attn_W_in"], np.float32)
    cw = np.asarray(inputs["L1_attn_conv_w"], np.float32)
    cb = np.asarray(inputs["L1_attn_conv_b"], np.float32)
    maps = []
    for c in range(NCORES):
        b, s = c // 4, c % 4
        sl = slice(s * 512, (s + 1) * 512)
        maps.append({
            "x2c": np.ascontiguousarray(x2b_full[b].T),
            "winx": np.ascontiguousarray(w_in[sl, :].T),
            "winz": np.ascontiguousarray(w_in[2048 + s * 512:2048 + (s + 1) * 512, :].T),
            "cw": np.ascontiguousarray(
                cw[sl, 0, :].reshape(4, 128, 4).transpose(1, 0, 2).reshape(128, 16)),
            "cb": _col4(cb[sl]),
        })
    return maps


def _l2b_in_maps(inputs, ub_full):
    w_x = np.asarray(inputs["L1_attn_W_x"], np.float32)
    maps = []
    for c in range(NCORES):
        b, s = c // 4, c % 4
        wsl = np.concatenate([w_x[s * 512:(s + 1) * 512], w_x[2048:2080]], axis=0)
        maps.append({
            "uf": np.ascontiguousarray(ub_full[b]),
            "wxT": np.ascontiguousarray(wsl.T),
        })
    return maps


def _l2c_in_maps(inputs, din_full, bc_b, r2a):
    w_dt = np.asarray(inputs["L1_attn_W_dt"], np.float32)
    b_dt = np.asarray(inputs["L1_attn_b_dt"], np.float32)
    alog = np.asarray(inputs["L1_attn_A_log"], np.float32)
    dd = np.asarray(inputs["L1_attn_D"], np.float32)
    w_out = np.asarray(inputs["L1_attn_W_out"], np.float32)
    ones_d = np.ones((1, 128), np.float32)
    ck = np.tile(np.array([list(CONSTS)], np.float32), (128, 1))
    maps = []
    for c in range(NCORES):
        b, s = c // 4, c % 4
        sl = slice(s * 512, (s + 1) * 512)
        maps.append({
            "dinf": np.ascontiguousarray(din_full[b]),
            "bcf": np.ascontiguousarray(bc_b[b]),
            "wdtT": np.ascontiguousarray(w_dt[sl, :].T),
            "bdt_c": _col4(b_dt[sl]),
            "alog_c": np.ascontiguousarray(
                alog[sl, :].reshape(4, 128, 16).transpose(1, 0, 2).reshape(128, 64)),
            "dcol": _col4(dd[sl]),
            "uf32": r2a[c]["u_f"],
            "zf32": r2a[c]["z_f"],
            "woT": np.ascontiguousarray(w_out[:, sl].T),
            "ckon": ck,
            "ones_d": ones_d,
        })
    return maps


def _l3_in_maps(inputs, po, r1):
    k0 = np.asarray(inputs["L1_ffn_k0"]).view(np.uint8)
    k1 = np.ascontiguousarray(np.asarray(inputs["L1_ffn_k1"]).view(np.uint8))
    k2 = np.ascontiguousarray(np.asarray(inputs["L1_ffn_k2"]).view(np.uint8))
    s0 = np.asarray(inputs["L1_ffn_s0"], np.float32).reshape(())
    s1 = np.asarray(inputs["L1_ffn_s1"], np.float32).reshape(())
    s2 = np.asarray(inputs["L1_ffn_s2"], np.float32).reshape(())
    scol = np.tile(np.stack([s0, s1, s2]).reshape(1, 3), (128, 1))
    lng = np.asarray(inputs["L1_attn_ln_g"], np.float32).reshape(1, D)
    lnb = np.asarray(inputs["L1_attn_ln_b"], np.float32).reshape(1, D)
    base = {
        "lng": lng, "lnb": lnb,
        "k0t": np.ascontiguousarray(k0[:D]),
        "k0b": np.ascontiguousarray(k0[D:]),
        "k1": k1, "k2": k2,
        "scol": np.ascontiguousarray(scol),
        "id128b": np.eye(128, dtype=np.float32).astype(BF),
        "ckon": np.tile(np.array([list(CONSTS)], np.float32), (128, 1)),
        "ones_d": np.ones((1, 128), np.float32),
    }
    maps = []
    for c in range(NCORES):
        b, off = c // 4, (c % 4) * T
        m = dict(base)
        for j in range(4):
            m[f"p{j}"] = np.ascontiguousarray(
                po[b * 4 + j][:, off:off + T].T)
        m["x2"] = r1[c]["xout"]
        maps.append(m)
    return maps


def kernel(**inputs):
    r1 = run_l1(inputs, moe=True)
    x2b_full = [np.concatenate([r1[b * 4 + j]["xout_bf"] for j in range(4)],
                               axis=0) for b in range(2)]
    r2a = _launch("l2a", build_l2a, _l2a_in_maps(inputs, x2b_full))
    ub_full = [np.concatenate([r2a[b * 4 + j]["u_b"] for j in range(4)], axis=0)
               for b in range(2)]
    r2b = _launch("l2b", build_l2b, _l2b_in_maps(inputs, ub_full))
    din_full = [np.concatenate([r2b[b * 4 + j]["din"] for j in range(4)], axis=0)
                for b in range(2)]
    bc_b = [r2b[b * 4]["bc"] for b in range(2)]
    r2c = _launch("l2c", build_l2c, _l2c_in_maps(inputs, din_full, bc_b, r2a))
    po = [r2c[c]["po"] for c in range(NCORES)]
    r3 = _launch("l3", build_l3, _l3_in_maps(inputs, po, r1))
    out = np.empty((2, TB, D), np.float32)
    for c in range(NCORES):
        b, off = c // 4, (c % 4) * T
        out[b, off:off + T] = r3[c]["xo"]
    return out
